# version 09
# baseline (speedup 1.0000x reference)
import numpy as np
import ml_dtypes
from contextlib import ExitStack

import concourse.bass as bass
import concourse.mybir as mybir
from concourse.bass_utils import run_bass_kernel_spmd

F32 = mybir.dt.float32
BF16 = mybir.dt.bfloat16
AF = mybir.ActivationFunctionType
ALU = mybir.AluOpType
AX = mybir.AxisListType
BF = ml_dtypes.bfloat16

D = 2048
QKV = 3 * D
HID = 5632
NW = HID * D
GW = NW // (8 * 128)
GQ = GW // 8
EPS_LN = 1e-5
EPS_ATT = 1e-15
EPS_Q = 1e-8



def build_A(T):
    NT = T // 128
    TS = T // 512
    nc = bass.Bass("TRN2", target_bir_lowering=False)

    x = nc.dram_tensor("x", [T, D], F32, kind="ExternalInput").ap()
    wqkv = nc.dram_tensor("wqkv", [QKV, D], F32, kind="ExternalInput").ap()
    gsh = nc.dram_tensor("gsh", [3, 128, GW], F32, kind="ExternalInput").ap()

    q_out = nc.dram_tensor("q_out", [D, T], BF16, kind="ExternalOutput").ap()
    vkT_out = nc.dram_tensor("vkT_out", [32, 2112], F32, kind="ExternalOutput").ap()
    gsum_out = nc.dram_tensor("gsum_out", [1, 3], F32, kind="ExternalOutput").ap()

    xn_scr = nc.dram_tensor("xn_scr", [T, D], BF16, kind="Internal").ap()
    wqkv_bf = nc.dram_tensor("wqkv_bf", [QKV, D], BF16, kind="Internal").ap()
    wqkvT = nc.dram_tensor("wqkvT", [D, QKV], BF16, kind="Internal").ap()
    k_scr = nc.dram_tensor("k_scr", [D, T], BF16, kind="Internal").ap()
    v_scr = nc.dram_tensor("v_scr", [D, T], BF16, kind="Internal").ap()

    ctx = ExitStack()
    sb = lambda n, sh, dt: ctx.enter_context(nc.sbuf_tensor(n, sh, dt))
    ps = lambda n, sh: ctx.enter_context(nc.psum_tensor(n, sh, F32))
    sem = lambda n: ctx.enter_context(nc.semaphore(n))
    psem = lambda n: [sem(n + "0"), sem(n + "1")]

    def pwait(eng, sp, idx):
        eng.wait_ge(sp[idx % 2], 16 * (idx // 2 + 1))

    def pall(eng, sp, n):
        if (n + 1) // 2:
            eng.wait_ge(sp[0], 16 * ((n + 1) // 2))
        if n // 2:
            eng.wait_ge(sp[1], 16 * (n // 2))

    xnf = sb("xnf", [128, 16 * T], BF16)
    lhsT = [sb(f"lhsT{p}", [128, 16, 128], BF16) for p in range(2)]
    ev = [sb(f"ev{p}", [128, T], BF16) for p in range(2)]
    wstrip = [sb(f"wstrip{p}", [128, QKV // 2], BF16) for p in range(2)]
    ones_f = sb("ones_f", [128, 1], F32)
    gcols = sb("gcols", [128, 28], F32)
    gev = sb("gev", [1, 3], F32)
    msum = sb("msum", [128, 2], F32)
    negm = sb("negm", [128, 2], F32)
    sqs = sb("sqs", [128, 2], F32)
    stdv = sb("stdv", [128, 2], F32)
    rstd = sb("rstd", [128, 2], F32)
    lnsc = sb("lnsc", [128, 1], F32)
    lneps = sb("lneps", [128, 1], F32)

    ph1 = ExitStack()
    sb1 = lambda n, sh, dt: ph1.enter_context(nc.sbuf_tensor(n, sh, dt))
    sx = [sb1(f"sx{p}", [128, D], F32) for p in range(2)]
    dump = sb1("dump", [128, D], BF16)
    xnb = [sb1(f"xnb{p}", [128, D], BF16) for p in range(2)]
    gst = sb1("gst", [128, GQ], F32)
    ph1.close()
    ktm = [sb(f"ktm{p}", [128, D], BF16) for p in range(2)]
    vtm = [sb(f"vtm{p}", [128, D], BF16) for p in range(2)]
    v33 = [sb(f"v33{p}", [128, 2112], BF16) for p in range(2)]
    vkev = sb("vkev", [128, 2112], F32)

    pp = [ps(f"pp{p}", [128, 512]) for p in range(2)]
    phg = ExitStack()
    gps = phg.enter_context(nc.psum_tensor("gps", [1, 3], F32))
    phg.close()
    vkps = ps("vkps", [128, 3072])

    def vk_slice(c):
        st = 512 * (c // 3) + 132 * (c % 3)
        return vkps[:, st:st + 132]

    s_gld = sem("s_gld"); s_gr = sem("s_gr"); s_gmm = sem("s_gmm"); s_gev = sem("s_gev")
    s_x = psem("s_x"); s_ctf = sem("s_ctf"); s_sq = sem("s_sq"); s_xnw = sem("s_xnw")
    s_xns = psem("s_xns"); s_xnf = sem("s_xnf")
    s_wb = sem("s_wb"); s_wts = psem("s_wts"); s_wtd = psem("s_wtd")
    s_lhs = psem("s_lhs"); s_mm = sem("s_mm"); s_mmc = sem("s_mmc"); s_evq = sem("s_evq")
    s_cst = psem("s_cst")
    s_kt = psem("s_kt"); s_vt = psem("s_vt"); s_v33 = sem("s_v33"); s_vkt = sem("s_vkt")
    s_vka = sem("s_vka"); s_vkst = sem("s_vkst"); s_gst = sem("s_gst")

    chunk_order = list(range(16, 48)) + list(range(0, 16))

    def chunk_dst(m):
        if m < 16:
            return q_out[128 * m:128 * (m + 1), :]
        if m < 32:
            return k_scr[128 * (m - 16):128 * (m - 15), :]
        return v_scr[128 * (m - 32):128 * (m - 31), :]

    with nc.Block() as blk:

        @blk.sync
        def _(sync):
            for i in range(NT):
                if i >= 2:
                    sync.wait_ge(s_xnw, i - 1)
                sync.dma_start(out=sx[i % 2][:], in_=x[128 * i:128 * (i + 1), :]
                               ).then_inc(s_x[i % 2], 16)
            for hs in range(32):
                kc, half = hs // 2, hs % 2
                pwait(sync, s_wts, hs)
                sync.dma_start(
                    out=wqkvT[128 * kc:128 * (kc + 1),
                              (QKV // 2) * half:(QKV // 2) * (half + 1)],
                    in_=wstrip[hs % 2][:]).then_inc(s_wtd[hs % 2], 16)
            pall(sync, s_wtd, 32)
            for j, m in enumerate(chunk_order):
                if j >= 2:
                    sync.wait_ge(s_mmc, j - 1)
                src = wqkvT[:, 128 * m:128 * (m + 1)].rearrange(
                    "(c p) m -> p c m", p=128)
                sync.dma_start(out=lhsT[j % 2][:], in_=src
                               ).then_inc(s_lhs[j % 2], 16)
                if j >= 1:
                    jj = j - 1
                    sync.wait_ge(s_evq, jj * TS + TS)
                    sync.dma_start(out=chunk_dst(chunk_order[jj]),
                                   in_=ev[jj % 2][:]).then_inc(s_cst[jj % 2], 16)
            jj = len(chunk_order) - 1
            sync.wait_ge(s_evq, jj * TS + TS)
            sync.dma_start(out=chunk_dst(chunk_order[jj]),
                           in_=ev[jj % 2][:]).then_inc(s_cst[jj % 2], 16)
            sync.wait_ge(s_gev, 1)
            sync.dma_start(out=gsum_out, in_=gev[:]).then_inc(s_gst, 16)
            sync.wait_ge(s_vka, 2 * NT)
            for jh in range(4):
                srcv = vkev[32 * jh:32 * (jh + 1), :].rearrange(
                    "d (c q) -> d c q", q=132)[:, :, 33 * jh:33 * (jh + 1)]
                dstv = vkT_out.rearrange(
                    "d (c q) -> d c q", q=132)[:, :, 33 * jh:33 * (jh + 1)]
                sync.dma_start(out=dstv, in_=srcv).then_inc(s_vkst, 16)
            pall(sync, s_cst, 48)
            sync.wait_ge(s_vkst, 64)
            sync.wait_ge(s_gst, 16)

        @blk.gpsimd
        def _(g):
            for cchunk in range(12):
                g.dma_start(out=wqkv_bf[512 * cchunk:512 * (cchunk + 1), :],
                            in_=wqkv[512 * cchunk:512 * (cchunk + 1), :]
                            ).then_inc(s_wb, 16)
            for i in range(NT):
                g.wait_ge(s_xnw, i + 1)
                g.dma_start(out=xn_scr[128 * i:128 * (i + 1), :],
                            in_=xnb[i % 2][:]).then_inc(s_xns[i % 2], 16)

        @blk.scalar
        def _(s):
            for gq in range(24):
                w_i, qq = gq // 8, gq % 8
                if gq >= 1:
                    s.wait_ge(s_gr, gq)
                s.dma_start(out=gst[:],
                            in_=gsh[w_i, :, GQ * qq:GQ * (qq + 1)]
                            ).then_inc(s_gld, 16)
            for i in range(NT):
                p = i % 2
                s.wait_ge(s_ctf, i + 1)
                s.activation(out=dump[:], in_=sx[p][:], func=AF.Square,
                             accum_out=sqs[:, p:p + 1])
                s.drain()
                s.activation(out=stdv[:, p:p + 1],
                             in_=sqs[:, p:p + 1], func=AF.Sqrt,
                             scale=lnsc[:, 0:1], bias=lneps[:, 0:1]
                             ).then_inc(s_sq, 1)
            s.wait_ge(s_wb, 16 * 12)
            for hs in range(32):
                kc, half = hs // 2, hs % 2
                if hs >= 2:
                    pwait(s, s_wtd, hs - 2)
                s.dma_start(out=wstrip[hs % 2][:],
                            in_=wqkv_bf[(QKV // 2) * half:(QKV // 2) * (half + 1),
                                        128 * kc:128 * (kc + 1)],
                            transpose=True).then_inc(s_wts[hs % 2], 16)
            pall(s, s_xns, NT)
            for c in range(16):
                s.dma_start(out=xnf[:, T * c:T * (c + 1)],
                            in_=xn_scr[:, 128 * c:128 * (c + 1)],
                            transpose=True).then_inc(s_xnf, 16)
            for j, m in enumerate(chunk_order):
                func = AF.Copy if 32 <= m < 48 else AF.Relu
                for tsub in range(TS):
                    gidx = j * TS + tsub
                    if j >= 2 and tsub == 0:
                        pwait(s, s_cst, j - 2)
                    s.wait_ge(s_mm, gidx + 1)
                    s.activation(out=ev[j % 2][:, 512 * tsub:512 * (tsub + 1)],
                                 in_=pp[gidx % 2][:], func=func
                                 ).then_inc(s_evq, 1)
            s.wait_ge(s_gev, 1)
            pall(s, s_cst, 32)
            for t in range(NT):
                if t >= 2:
                    s.wait_ge(s_vkt, t - 1)
                s.dma_start(out=ktm[t % 2][:], in_=k_scr[:, 128 * t:128 * (t + 1)],
                            transpose=True).then_inc(s_kt[t % 2], 16)
                s.dma_start(out=vtm[t % 2][:], in_=v_scr[:, 128 * t:128 * (t + 1)],
                            transpose=True).then_inc(s_vt[t % 2], 16)

        @blk.vector
        def _(v):
            v.memset(ones_f[:], 1.0)
            v.memset(lnsc[:], 1.0 / D)
            v.memset(lneps[:], EPS_LN)
            v.drain()
            for gq in range(24):
                w_i = gq // 8
                v.wait_ge(s_gld, 16 * (gq + 1))
                v.reduce_sum(out=gcols[:, gq:gq + 1], in_=gst[:],
                             axis=AX.X, apply_absolute_value=True
                             ).then_inc(s_gr, 1)
            v.wait_ge(s_gr, 24)
            for w_i in range(3):
                v.reduce_sum(out=gcols[:, 24 + w_i:25 + w_i],
                             in_=gcols[:, 8 * w_i:8 * (w_i + 1)], axis=AX.X
                             ).then_inc(s_gr, 1)
            v.wait_ge(s_gmm, 1)
            v.tensor_copy(out=gev[:], in_=gps[:]).then_inc(s_gev, 1)
            for i in range(NT):
                p = i % 2
                pwait(v, s_x, i)
                if i >= 2:
                    v.wait_ge(s_sq, i - 1)
                    pwait(v, s_xns, i - 2)
                v.reduce_sum(out=msum[:, p:p + 1], in_=sx[p][:], axis=AX.X)
                v.drain()
                v.tensor_scalar_mul(out=negm[:, p:p + 1], in0=msum[:, p:p + 1],
                                    scalar1=-1.0 / D)
                v.drain()
                v.tensor_scalar_add(out=sx[p][:], in0=sx[p][:],
                                    scalar1=negm[:, p:p + 1]).then_inc(s_ctf, 1)
                v.wait_ge(s_sq, i + 1)
                v.reciprocal(out=rstd[:, p:p + 1], in_=stdv[:, p:p + 1])
                v.drain()
                v.tensor_scalar_mul(out=xnb[p][:], in0=sx[p][:],
                                    scalar1=rstd[:, p:p + 1]).then_inc(s_xnw, 1)
            pall(v, s_xns, NT)
            for p in range(2):
                v.memset(v33[p][:].rearrange("p (h e) -> p h e", e=33)[:, :, 32:33], 1.0)
            v.memset(vkev[:], 0.0)
            v.drain()
            for t in range(NT):
                pwait(v, s_vt, t)
                if t >= 2:
                    v.wait_ge(s_vkt, t - 1)
                v.tensor_copy(
                    out=v33[t % 2][:].rearrange("p (h e) -> p h e", e=33)[:, :, 0:32],
                    in_=vtm[t % 2][:].rearrange("p (h e) -> p h e", e=32),
                ).then_inc(s_v33, 1)
                v.wait_ge(s_vkt, t + 1)
                v.drain()
                v.tensor_add(
                    out=vkev[:, 0:1980].rearrange("p (b g) -> p b g", g=396),
                    in0=vkps[:].rearrange("p (b g) -> p b g", g=512)[:, 0:5, 0:396],
                    in1=vkev[:, 0:1980].rearrange("p (b g) -> p b g", g=396),
                ).then_inc(s_vka, 1)
                v.tensor_add(out=vkev[:, 1980:2112], in0=vkps[:, 2560:2692],
                             in1=vkev[:, 1980:2112]).then_inc(s_vka, 1)

        @blk.tensor
        def _(t_):
            t_.wait_ge(s_gr, 27)
            t_.matmul(gps[:], ones_f[:], gcols[:, 24:27], start=True, stop=True
                      ).then_inc(s_gmm, 1)
            t_.wait_ge(s_xnf, 256)
            for j, m in enumerate(chunk_order):
                pwait(t_, s_lhs, j)
                for tsub in range(TS):
                    gidx = j * TS + tsub
                    if gidx >= 2:
                        t_.wait_ge(s_evq, gidx - 1)
                    for kc in range(16):
                        mm = t_.matmul(pp[gidx % 2][:],
                                       lhsT[j % 2][:, kc, :],
                                       xnf[:, T * kc + 512 * tsub:
                                           T * kc + 512 * (tsub + 1)],
                                       start=(kc == 0), stop=(kc == 15))
                    mm.then_inc(s_mm, 1)
                    if tsub == TS - 1:
                        t_.nop().then_inc(s_mmc, 1)
                if j == 31:
                    t_.wait_ge(s_gev, 1)
            for t in range(NT):
                pwait(t_, s_kt, t)
                t_.wait_ge(s_v33, t + 1)
                if t >= 1:
                    t_.wait_ge(s_vka, 2 * t)
                for c in range(16):
                    mm = t_.matmul(vk_slice(c),
                                   ktm[t % 2][:, 128 * c:128 * (c + 1)],
                                   v33[t % 2][:, 132 * c:132 * (c + 1)],
                                   start=True, stop=True)
                mm.then_inc(s_vkt, 1)

    ctx.close()
    return nc



def build_B(T):
    NT = T // 128
    TS = T // 512
    W12 = HID // 128
    nc = bass.Bass("TRN2", target_bir_lowering=False)

    x = nc.dram_tensor("x", [T, D], F32, kind="ExternalInput").ap()
    q_in = nc.dram_tensor("q_in", [D, T], BF16, kind="ExternalInput").ap()
    vkp = nc.dram_tensor("vkp", [8, 32, 2112], F32, kind="ExternalInput").ap()
    bselw = nc.dram_tensor("bselw", [32, 8], F32, kind="ExternalInput").ap()
    gsums = nc.dram_tensor("gsums", [8, 4], F32, kind="ExternalInput").ap()
    ind = nc.dram_tensor("ind", [64, 2048], BF16, kind="ExternalInput").ap()
    w1 = nc.dram_tensor("w1", [HID, D], F32, kind="ExternalInput").ap()
    w2 = nc.dram_tensor("w2", [HID, D], F32, kind="ExternalInput").ap()
    w3 = nc.dram_tensor("w3", [D, HID], F32, kind="ExternalInput").ap()
    wproj = nc.dram_tensor("wproj", [D, D], F32, kind="ExternalInput").ap()
    bproj = nc.dram_tensor("bproj", [1, D], F32, kind="ExternalInput").ap()

    out = nc.dram_tensor("out", [T, D], F32, kind="ExternalOutput").ap()

    vk_dram = nc.dram_tensor("vk_dram", [32, 2112], F32, kind="Internal").ap()
    mix_scr = nc.dram_tensor("mix_scr", [D, T], BF16, kind="Internal").ap()
    xn_scr = nc.dram_tensor("xn_scr", [T, D], BF16, kind="Internal").ap()
    hid_scr = nc.dram_tensor("hid_scr", [HID, T], BF16, kind="Internal").ap()
    w1q_bf = nc.dram_tensor("w1q_bf", [HID, D], BF16, kind="Internal").ap()
    w2q_bf = nc.dram_tensor("w2q_bf", [HID, D], BF16, kind="Internal").ap()
    w3q_bf = nc.dram_tensor("w3q_bf", [D, HID], BF16, kind="Internal").ap()
    wproj_bf = nc.dram_tensor("wproj_bf", [D, D], BF16, kind="Internal").ap()
    w1qT = nc.dram_tensor("w1qT", [D, HID], BF16, kind="Internal").ap()
    w2qT = nc.dram_tensor("w2qT", [D, HID], BF16, kind="Internal").ap()
    w3qT = nc.dram_tensor("w3qT", [HID, D], BF16, kind="Internal").ap()
    wprojT = nc.dram_tensor("wprojT", [D, D], BF16, kind="Internal").ap()

    ctx = ExitStack()
    sb = lambda n, sh, dt: ctx.enter_context(nc.sbuf_tensor(n, sh, dt))
    ps = lambda n, sh: ctx.enter_context(nc.psum_tensor(n, sh, F32))
    sem = lambda n: ctx.enter_context(nc.semaphore(n))
    psem = lambda n: [sem(n + "0"), sem(n + "1")]

    def pwait(eng, sp, idx, k=16):
        eng.wait_ge(sp[idx % 2], k * (idx // 2 + 1))

    def pall(eng, sp, n, k=16):
        if (n + 1) // 2:
            eng.wait_ge(sp[0], k * ((n + 1) // 2))
        if n // 2:
            eng.wait_ge(sp[1], k * (n // 2))

    ones_f = sb("ones_f", [128, 1], F32)
    ones_r = sb("ones_r", [1, 128], F32)
    ones_bf = sb("ones_bf", [1, 128], BF16)
    gsb = sb("gsb", [8, 4], F32)
    gtot = sb("gtot", [1, 4], F32)
    thr = sb("thr", [128, 4], F32)
    nthr = sb("nthr", [128, 4], F32)
    ind_sb = sb("ind_sb", [64, 2048], BF16)
    bpr = sb("bpr", [1, D], BF16)
    bselw_sb = sb("bselw_sb", [32, 8], F32)
    lhs_num = sb("lhs_num", [128, 16, 128], BF16)
    lhs_den = sb("lhs_den", [128, 1024], BF16)
    vkacc = sb("vkacc", [32, 2112], F32)

    ph01 = ExitStack()
    sA = lambda n, sh, dt: ph01.enter_context(nc.sbuf_tensor(n, sh, dt))
    vkt_b = [sA(f"vkt{p}", [32, 2112], F32) for p in range(2)]
    qt = [sA(f"qt{p}", [128, 16, 512], BF16) for p in range(2)]
    mix_sb = [sA(f"mix_sb{p}", [128, 16, 512], BF16) for p in range(2)]
    den_f = sA("den_f", [64, 512], F32)
    den_r = sA("den_r", [64, 512], F32)
    den_b = sA("den_b", [64, 512], BF16)
    bc_sb = [sA(f"bc{p}", [128, 512], BF16) for p in range(2)]
    wfA = [sA(f"wfA{p}", [128, 2048], F32) for p in range(2)]
    wltA = sA("wltA", [128, 2048], BF16)
    wqA = [sA(f"wqA{p}", [128, 2048], BF16) for p in range(2)]
    wsA = [sA(f"wsA{p}", [128, HID // 2], BF16) for p in range(2)]
    ph01.close()

    phx = ExitStack()
    xnf = phx.enter_context(nc.sbuf_tensor("xnf", [128, 16 * T], BF16))

    ph2 = ExitStack()
    sB = lambda n, sh, dt: ph2.enter_context(nc.sbuf_tensor(n, sh, dt))
    sx = [sB(f"sx{p}", [128, D], F32) for p in range(2)]
    dump = sB("dump", [128, D], BF16)
    xnb = [sB(f"xnb{p}", [128, D], BF16) for p in range(2)]
    msum = sB("msum", [128, 2], F32)
    negm = sB("negm", [128, 2], F32)
    sqs = sB("sqs", [128, 2], F32)
    stdv = sB("stdv", [128, 2], F32)
    rstd = sB("rstd", [128, 2], F32)
    lnsc = sB("lnsc", [128, 1], F32)
    lneps = sB("lneps", [128, 1], F32)
    wfB = [sB(f"wfB{p}", [128, 1408], F32) for p in range(2)]
    wltB = sB("wltB", [128, 1408], BF16)
    wqB = [sB(f"wqB{p}", [128, 1408], BF16) for p in range(2)]
    wsB = [sB(f"wsB{p}", [128, 2048], BF16) for p in range(2)]
    ph2.close()

    ph3 = ExitStack()
    sC = lambda n, sh, dt: ph3.enter_context(nc.sbuf_tensor(n, sh, dt))
    l1 = [sC(f"l1_{p}", [128, 16, 128], BF16) for p in range(2)]
    l2 = [sC(f"l2_{p}", [128, 16, 128], BF16) for p in range(2)]
    sig = [sC(f"sig{p}", [128, 512], BF16) for p in range(2)]
    hid_sb = [sC(f"hid{p}", [128, T], BF16) for p in range(2)]
    ph3.close()
    phx.close()

    ph4 = ExitStack()
    sD = lambda n, sh, dt: ph4.enter_context(nc.sbuf_tensor(n, sh, dt))
    w3r = sD("w3r", [128, 44, 1024], BF16)
    wpr = sD("wpr", [128, 16, 1024], BF16)
    mixt = [sD("mixt0", [128, 16, 512], BF16)] * 2
    hidl = [sD(f"hidl{p}", [128, 44, 128], BF16) for p in range(2)]
    xs = [sD(f"xs{p}", [128, 1024], F32) for p in range(2)]
    osb = [sD(f"osb{p}", [128, 1024], F32) for p in range(2)]
    ph4.close()

    phg = ExitStack()
    gpsA = phg.enter_context(nc.psum_tensor("gpsA", [1, 4], F32))
    phg.close()
    dps = [ps(f"dps{p}", [64, 512]) for p in range(2)]
    nps = [ps(f"nps{p}", [128, 512]) for p in range(2)]
    bps = [ps(f"bps{p}", [128, 512]) for p in range(2)]
    ops_ = [ps(f"ops{p}", [128, 512]) for p in range(2)]

    s_ph = [sem(f"s_ph{k}") for k in range(5)]
    s_ini = sem("s_ini")
    s_gld = sem("s_gld"); s_ind = sem("s_ind"); s_bsl = sem("s_bsl")
    s_bpr = sem("s_bpr")
    s_g1 = sem("s_g1"); s_g2 = sem("s_g2"); s_g3 = sem("s_g3")
    s_g4 = sem("s_g4"); s_g5 = sem("s_g5"); s_g6 = sem("s_g6")
    s_vl = psem("s_vl"); s_vac = sem("s_vac"); s_vkd = sem("s_vkd")
    s_lnb = sem("s_lnb"); s_ldz = sem("s_ldz")
    s_qt = psem("s_qt"); s_dmm = sem("s_dmm"); s_den = sem("s_den")
    s_nmm = sem("s_nmm"); s_bmm = sem("s_bmm"); s_bev = sem("s_bev")
    s_mxc = sem("s_mxc"); s_mxs = psem("s_mxs")
    s_wld = psem("s_wld"); s_wq = sem("s_wq")
    s_wqs = psem("s_wqs"); s_wts = psem("s_wts"); s_wtd = psem("s_wtd")
    s_x = psem("s_x"); s_ctf = sem("s_ctf"); s_sq = sem("s_sq"); s_xnw = sem("s_xnw")
    s_xns = psem("s_xns"); s_xnf = sem("s_xnf")
    s_w3ld = psem("s_w3ld"); s_w3q = sem("s_w3q")
    s_w3qs = psem("s_w3qs"); s_w3ts = psem("s_w3ts"); s_w3td = psem("s_w3td")
    s_wpc = sem("s_wpc"); s_wpts = psem("s_wpts"); s_wptd = psem("s_wptd")
    s_l1 = psem("s_l1"); s_l2 = psem("s_l2")
    s_h1 = sem("s_h1"); s_h2 = sem("s_h2"); s_sg = sem("s_sg"); s_hm = sem("s_hm")
    s_hst = psem("s_hst")
    s_res = sem("s_res")
    s_mxt = psem("s_mxt"); s_hl = psem("s_hl"); s_xl = psem("s_xl")
    s_omm = sem("s_omm"); s_oev = sem("s_oev"); s_ost = psem("s_ost")

    NENG = 5

    def phase_end(eng, k):
        eng.drain()
        eng.nop().then_inc(s_ph[k], 1)

    def phase_wait(eng, k):
        eng.wait_ge(s_ph[k], NENG * (k + 1))

    def phase_wait1(eng, k):
        eng.wait_ge(s_ph[k], NENG)

    with nc.Block() as blk:

        @blk.sync
        def _(sync):
            sync.dma_start(out=gsb[:], in_=gsums).then_inc(s_gld, 16)
            sync.dma_start(out=ind_sb[:], in_=ind).then_inc(s_ind, 16)
            sync.dma_start(out=bselw_sb[:], in_=bselw).then_inc(s_bsl, 16)
            for r in range(8):
                if r >= 2:
                    sync.wait_ge(s_vac, r - 1)
                sync.dma_start(out=vkt_b[r % 2][:], in_=vkp[r, :, :]
                               ).then_inc(s_vl[r % 2], 16)
            sync.wait_ge(s_vac, 8)
            sync.dma_start(out=vk_dram, in_=vkacc[:]).then_inc(s_vkd, 16)
            sync.wait_ge(s_vkd, 16)
            sync.wait_ge(s_gld, 16)
            sync.wait_ge(s_ind, 16)
            sync.wait_ge(s_bsl, 16)
            phase_end(sync, 0)
            phase_wait1(sync, 0)
            for tt in range(TS):
                if tt >= 2:
                    sync.wait_ge(s_nmm, 16 * (tt - 1))
                sync.dma_start(
                    out=qt[tt % 2][:],
                    in_=q_in[:, 512 * tt:512 * (tt + 1)].rearrange(
                        "(c p) n -> p c n", p=128)).then_inc(s_qt[tt % 2], 16)
                if tt >= 1:
                    sync.wait_ge(s_mxc, 16 * tt)
                    sync.dma_start(
                        out=mix_scr[:, 512 * (tt - 1):512 * tt].rearrange(
                            "(c p) n -> p c n", p=128),
                        in_=mix_sb[(tt - 1) % 2][:]).then_inc(s_mxs[(tt - 1) % 2], 16)
            sync.wait_ge(s_mxc, 16 * TS)
            sync.dma_start(
                out=mix_scr[:, 512 * (TS - 1):512 * TS].rearrange(
                    "(c p) n -> p c n", p=128),
                in_=mix_sb[(TS - 1) % 2][:]).then_inc(s_mxs[(TS - 1) % 2], 16)
            for g in range(2 * 32):
                w_i, hs = g // 32, g % 32
                kc, half = hs // 2, hs % 2
                wTd = w1qT if w_i == 0 else w2qT
                pwait(sync, s_wts, g)
                sync.dma_start(
                    out=wTd[128 * kc:128 * (kc + 1),
                            (HID // 2) * half:(HID // 2) * (half + 1)],
                    in_=wsA[g % 2][:]).then_inc(s_wtd[g % 2], 16)
            pall(sync, s_mxs, TS)
            pall(sync, s_wtd, 64)
            phase_end(sync, 1)
            phase_wait1(sync, 1)
            for i in range(NT):
                if i >= 2:
                    sync.wait_ge(s_xnw, i - 1)
                sync.dma_start(out=sx[i % 2][:], in_=x[128 * i:128 * (i + 1), :]
                               ).then_inc(s_x[i % 2], 16)
            for g in range(44):
                pwait(sync, s_w3ts, g)
                sync.dma_start(out=w3qT[128 * g:128 * (g + 1), :],
                               in_=wsB[g % 2][:]).then_inc(s_w3td[g % 2], 16)
            for g in range(16):
                pwait(sync, s_wpts, g)
                sync.dma_start(out=wprojT[128 * g:128 * (g + 1), :],
                               in_=wsB[g % 2][:]).then_inc(s_wptd[g % 2], 16)
            pall(sync, s_xns, NT)
            pall(sync, s_w3td, 44)
            pall(sync, s_wptd, 16)
            phase_end(sync, 2)
            phase_wait1(sync, 2)
            for h in range(W12):
                if h >= 2:
                    sync.wait_ge(s_h2, TS * (h - 1))
                sync.dma_start(
                    out=l1[h % 2][:],
                    in_=w1qT[:, 128 * h:128 * (h + 1)].rearrange(
                        "(c p) m -> p c m", p=128)).then_inc(s_l1[h % 2], 16)
                sync.dma_start(
                    out=l2[h % 2][:],
                    in_=w2qT[:, 128 * h:128 * (h + 1)].rearrange(
                        "(c p) m -> p c m", p=128)).then_inc(s_l2[h % 2], 16)
                if h >= 1:
                    hh = h - 1
                    sync.wait_ge(s_hm, TS * (hh + 1))
                    sync.dma_start(out=hid_scr[128 * hh:128 * (hh + 1), :],
                                   in_=hid_sb[hh % 2][:]).then_inc(s_hst[hh % 2], 16)
            hh = W12 - 1
            sync.wait_ge(s_hm, TS * (hh + 1))
            sync.dma_start(out=hid_scr[128 * hh:128 * (hh + 1), :],
                           in_=hid_sb[hh % 2][:]).then_inc(s_hst[hh % 2], 16)
            pall(sync, s_hst, W12)
            phase_end(sync, 3)
            phase_wait1(sync, 3)
            for mh in range(2):
                if mh == 1:
                    sync.wait_ge(s_omm, 4 * TS)
                for g in range(44):
                    sync.dma_start(
                        out=w3r[:, g, :],
                        in_=w3qT[128 * g:128 * (g + 1),
                                 1024 * mh:1024 * (mh + 1)]).then_inc(s_res, 16)
                for g in range(16):
                    sync.dma_start(
                        out=wpr[:, g, :],
                        in_=wprojT[128 * g:128 * (g + 1),
                                   1024 * mh:1024 * (mh + 1)]).then_inc(s_res, 16)
                for tt in range(TS):
                    gmt = mh * TS + tt
                    if gmt >= 1:
                        sync.wait_ge(s_omm, 4 * gmt)
                    sync.dma_start(
                        out=mixt[gmt % 2][:],
                        in_=mix_scr[:, 512 * tt:512 * (tt + 1)].rearrange(
                            "(c p) n -> p c n", p=128)).then_inc(s_mxt[gmt % 2], 16)
                    for su in range(4):
                        it = gmt * 4 + su
                        tok = tt * 4 + su
                        if it >= 2:
                            sync.wait_ge(s_omm, it - 1)
                        srch = hid_scr[:, 128 * tok:128 * (tok + 1)].rearrange(
                            "(c p) n -> p c n", p=128)
                        for part in range(4):
                            sync.dma_start(out=hidl[it % 2][:, 11 * part:11 * (part + 1), :],
                                           in_=srch[:, 11 * part:11 * (part + 1), :]
                                           ).then_inc(s_hl[it % 2], 16)
                        if it >= 2:
                            sync.wait_ge(s_oev, it - 1)
                        sync.dma_start(out=xs[it % 2][:],
                                       in_=x[128 * tok:128 * (tok + 1),
                                             1024 * mh:1024 * (mh + 1)]
                                       ).then_inc(s_xl[it % 2], 16)
                        if it >= 1:
                            ip = it - 1
                            mhp = ip // (4 * TS)
                            tokp = ip % (4 * TS)
                            sync.wait_ge(s_oev, ip + 1)
                            sync.dma_start(
                                out=out[128 * tokp:128 * (tokp + 1),
                                        1024 * mhp:1024 * (mhp + 1)],
                                in_=osb[ip % 2][:]).then_inc(s_ost[ip % 2], 16)
            ip = 8 * TS - 1
            sync.wait_ge(s_oev, ip + 1)
            sync.dma_start(out=out[128 * (4 * TS - 1):128 * 4 * TS, 1024:2048],
                           in_=osb[ip % 2][:]).then_inc(s_ost[ip % 2], 16)
            pall(sync, s_ost, 8 * TS)

        @blk.gpsimd
        def _(g):
            g.dma_start(out=bpr[:], in_=bproj).then_inc(s_bpr, 16)
            g.wait_ge(s_vkd, 16)
            g.wait_ge(s_ldz, 1)
            for jh in range(4):
                srcn = vk_dram.rearrange("d (c q) -> d c q", q=132)[
                    :, :, 33 * jh:33 * jh + 32]
                g.dma_start(out=lhs_num[32 * jh:32 * (jh + 1), :,
                                         32 * jh:32 * (jh + 1)], in_=srcn
                            ).then_inc(s_lnb, 16)
                srcd = vk_dram[:, 33 * jh + 32::132][:, 0:16]
                flat = lhs_den[32 * jh:32 * (jh + 1), jh:1024:68][:, 0:16]
                g.dma_start(out=flat, in_=srcd,
                            allow_slow_non_contiguous=True).then_inc(s_lnb, 16)
            g.wait_ge(s_lnb, 16 * 8)
            g.wait_ge(s_bpr, 16)
            phase_end(g, 0)
            phase_wait1(g, 0)
            for k in range(2 * W12):
                hc = k % W12
                if k >= 2:
                    g.wait_ge(s_wq, k - 1)
                g.dma_start(out=wfA[k % 2][:],
                            in_=(w1 if k < W12 else w2)[128 * hc:128 * (hc + 1), :]
                            ).then_inc(s_wld[k % 2], 16)
                if k >= 1:
                    kk = k - 1
                    wdst = w1q_bf if kk < W12 else w2q_bf
                    hk = kk % W12
                    g.wait_ge(s_wq, kk + 1)
                    g.dma_start(out=wdst[128 * hk:128 * (hk + 1), :],
                                in_=wqA[kk % 2][:]).then_inc(s_wqs[kk % 2], 16)
            kk = 2 * W12 - 1
            g.wait_ge(s_wq, kk + 1)
            g.dma_start(out=w2q_bf[128 * (W12 - 1):128 * W12, :],
                        in_=wqA[kk % 2][:]).then_inc(s_wqs[kk % 2], 16)
            phase_end(g, 1)
            phase_wait1(g, 1)
            g.dma_start(out=wproj_bf, in_=wproj).then_inc(s_wpc, 16)
            for k3 in range(64):
                if k3 >= 2:
                    g.wait_ge(s_w3q, k3 - 1)
                g.dma_start(
                    out=wfB[k3 % 2][:],
                    in_=w3[128 * (k3 // 4):128 * (k3 // 4 + 1),
                           1408 * (k3 % 4):1408 * (k3 % 4 + 1)]
                    ).then_inc(s_w3ld[k3 % 2], 16)
                if k3 >= 1:
                    kk3 = k3 - 1
                    g.wait_ge(s_w3q, kk3 + 1)
                    g.dma_start(
                        out=w3q_bf[128 * (kk3 // 4):128 * (kk3 // 4 + 1),
                                   1408 * (kk3 % 4):1408 * (kk3 % 4 + 1)],
                        in_=wqB[kk3 % 2][:]).then_inc(s_w3qs[kk3 % 2], 16)
            kk3 = 63
            g.wait_ge(s_w3q, 64)
            g.dma_start(out=w3q_bf[128 * 15:128 * 16, 1408 * 3:5632],
                        in_=wqB[kk3 % 2][:]).then_inc(s_w3qs[kk3 % 2], 16)
            g.wait_ge(s_wpc, 16)
            phase_end(g, 2)
            phase_wait1(g, 2)
            phase_end(g, 3)
            phase_wait1(g, 3)

        @blk.scalar
        def _(s):
            s.wait_ge(s_g1, 1)
            s.activation(out=gtot[:], in_=gpsA[:], func=AF.Copy).then_inc(s_g2, 1)
            s.wait_ge(s_g4, 1)
            s.activation(out=thr[:], in_=nps[0][:, 0:4], func=AF.Copy
                         ).then_inc(s_g5, 1)
            phase_end(s, 0)
            phase_wait1(s, 0)
            for tt in range(TS):
                for c in range(16):
                    gi = tt * 16 + c
                    s.wait_ge(s_bmm, gi + 1)
                    if gi >= 2:
                        s.wait_ge(s_mxc, gi - 1)
                    s.activation(out=bc_sb[gi % 2][:], in_=bps[gi % 2][:],
                                 func=AF.Copy).then_inc(s_bev, 1)
            for g in range(2 * 32):
                w_i, hs = g // 32, g % 32
                kc, half = hs // 2, hs % 2
                wqb = w1q_bf if w_i == 0 else w2q_bf
                if hs == 0:
                    pall(s, s_wqs, W12 * (w_i + 1))
                if g >= 2:
                    pwait(s, s_wtd, g - 2)
                s.dma_start(out=wsA[g % 2][:],
                            in_=wqb[(HID // 2) * half:(HID // 2) * (half + 1),
                                    128 * kc:128 * (kc + 1)],
                            transpose=True).then_inc(s_wts[g % 2], 16)
            phase_end(s, 1)
            phase_wait1(s, 1)
            for i in range(NT):
                p = i % 2
                s.wait_ge(s_ctf, i + 1)
                s.activation(out=dump[:], in_=sx[p][:], func=AF.Square,
                             accum_out=sqs[:, p:p + 1])
                s.drain()
                s.activation(out=stdv[:, p:p + 1], in_=sqs[:, p:p + 1],
                             func=AF.Sqrt, scale=lnsc[:, 0:1],
                             bias=lneps[:, 0:1]).then_inc(s_sq, 1)
                s.wait_ge(s_xnw, i + 1)
                s.dma_start(out=xn_scr[128 * i:128 * (i + 1), :],
                            in_=xnb[i % 2][:]).then_inc(s_xns[i % 2], 16)
            pall(s, s_xns, NT)
            for c in range(16):
                s.dma_start(out=xnf[:, T * c:T * (c + 1)],
                            in_=xn_scr[:, 128 * c:128 * (c + 1)],
                            transpose=True).then_inc(s_xnf, 16)
            pall(s, s_w3qs, 64)
            for g in range(44):
                if g >= 2:
                    pwait(s, s_w3td, g - 2)
                s.dma_start(out=wsB[g % 2][:],
                            in_=w3q_bf[:, 128 * g:128 * (g + 1)],
                            transpose=True).then_inc(s_w3ts[g % 2], 16)
            s.wait_ge(s_wpc, 16)
            pwait(s, s_w3td, 42)
            pwait(s, s_w3td, 43)
            for g in range(16):
                if g >= 2:
                    pwait(s, s_wptd, g - 2)
                s.dma_start(out=wsB[g % 2][:],
                            in_=wproj_bf[:, 128 * g:128 * (g + 1)],
                            transpose=True).then_inc(s_wpts[g % 2], 16)
            phase_end(s, 2)
            phase_wait1(s, 2)
            for h in range(W12):
                for tsub in range(TS):
                    gi = h * TS + tsub
                    s.wait_ge(s_h1, gi + 1)
                    if gi >= 2:
                        s.wait_ge(s_hm, gi - 1)
                    s.activation(out=sig[gi % 2][:], in_=nps[gi % 2][:],
                                 func=AF.Sigmoid).then_inc(s_sg, 1)
            phase_end(s, 3)
            phase_wait1(s, 3)

        @blk.vector
        def _(v):
            v.memset(ones_f[:], 1.0)
            v.memset(ones_r[:], 1.0)
            v.memset(ones_bf[:], 1.0)
            v.memset(vkacc[:], 0.0)
            v.memset(lhs_den[:], 0.0)
            v.memset(lhs_num[:], 0.0)
            v.drain()
            v.nop().then_inc(s_ini, 1)
            v.nop().then_inc(s_ldz, 1)
            v.wait_ge(s_bsl, 16)
            for r in range(8):
                pwait(v, s_vl, r)
                v.drain()
                v.scalar_tensor_tensor(out=vkacc[:], in0=vkt_b[r % 2][:],
                                       scalar=bselw_sb[:, r:r + 1],
                                       in1=vkacc[:], op0=ALU.mult,
                                       op1=ALU.add).then_inc(s_vac, 1)
            v.wait_ge(s_g2, 1)
            v.tensor_scalar(out=gtot[:, 0:3], in0=gtot[:, 0:3],
                            scalar1=0.5 / NW, scalar2=0.5 * EPS_Q,
                            op0=ALU.mult, op1=ALU.add).then_inc(s_g3, 1)
            v.wait_ge(s_g5, 1)
            v.tensor_scalar_mul(out=nthr[:], in0=thr[:], scalar1=-1.0
                                ).then_inc(s_g6, 1)
            phase_end(v, 0)
            phase_wait1(v, 0)
            v_st = [0]

            def emit_stt(budget):
                for _ in range(budget):
                    k = v_st[0]
                    if k >= 2 * W12:
                        return
                    pwait(v, s_wld, k)
                    if k >= 2:
                        pwait(v, s_wqs, k - 2)
                    v.drain()
                    v.tensor_scalar(out=wltA[:], in0=wfA[k % 2][:],
                                    scalar1=nthr[:, k // W12:k // W12 + 1],
                                    scalar2=None, op0=ALU.is_lt)
                    v.drain()
                    v.scalar_tensor_tensor(
                        out=wqA[k % 2][:], in0=wfA[k % 2][:],
                        scalar=thr[:, k // W12:k // W12 + 1],
                        in1=wltA[:], op0=ALU.is_gt, op1=ALU.subtract
                        ).then_inc(s_wq, 1)
                    v_st[0] += 1

            for tt in range(TS):
                v.wait_ge(s_dmm, tt + 1)
                if tt >= 1:
                    v.wait_ge(s_bmm, 16 * tt)
                v.drain()
                v.tensor_scalar(out=den_f[:], in0=dps[tt % 2][:], scalar1=EPS_ATT,
                                scalar2=None, op0=ALU.add)
                v.drain()
                v.reciprocal(out=den_r[:], in_=den_f[:])
                v.drain()
                v.tensor_copy(out=den_b[:], in_=den_r[:]).then_inc(s_den, 1)
                for c in range(16):
                    gi = tt * 16 + c
                    v.wait_ge(s_nmm, gi + 1)
                    v.wait_ge(s_bev, gi + 1)
                    if c == 0 and tt >= 2:
                        pwait(v, s_mxs, tt - 2)
                    v.tensor_mul(out=mix_sb[tt % 2][:, c, :], in0=nps[gi % 2][:],
                                 in1=bc_sb[gi % 2][:]).then_inc(s_mxc, 1)
                emit_stt(12)
            emit_stt(2 * W12)
            phase_end(v, 1)
            phase_wait1(v, 1)
            v.memset(lnsc[:], 1.0 / D)
            v.memset(lneps[:], EPS_LN)
            v.drain()
            v3_st = [0]

            def emit_stt3(budget):
                for _ in range(budget):
                    k3 = v3_st[0]
                    if k3 >= 64:
                        return
                    pwait(v, s_w3ld, k3)
                    if k3 >= 2:
                        pwait(v, s_w3qs, k3 - 2)
                    v.drain()
                    v.tensor_scalar(out=wltB[:], in0=wfB[k3 % 2][:],
                                    scalar1=nthr[:, 2:3], scalar2=None,
                                    op0=ALU.is_lt)
                    v.drain()
                    v.scalar_tensor_tensor(
                        out=wqB[k3 % 2][:], in0=wfB[k3 % 2][:], scalar=thr[:, 2:3],
                        in1=wltB[:], op0=ALU.is_gt, op1=ALU.subtract
                        ).then_inc(s_w3q, 1)
                    v3_st[0] += 1

            for i in range(NT):
                p = i % 2
                pwait(v, s_x, i)
                if i >= 2:
                    v.wait_ge(s_sq, i - 1)
                    pwait(v, s_xns, i - 2)
                v.reduce_sum(out=msum[:, p:p + 1], in_=sx[p][:], axis=AX.X)
                v.drain()
                v.tensor_scalar_mul(out=negm[:, p:p + 1], in0=msum[:, p:p + 1],
                                    scalar1=-1.0 / D)
                v.drain()
                v.tensor_scalar_add(out=sx[p][:], in0=sx[p][:],
                                    scalar1=negm[:, p:p + 1]).then_inc(s_ctf, 1)
                v.wait_ge(s_sq, i + 1)
                v.reciprocal(out=rstd[:, p:p + 1], in_=stdv[:, p:p + 1])
                v.drain()
                v.tensor_scalar_mul(out=xnb[p][:], in0=sx[p][:],
                                    scalar1=rstd[:, p:p + 1]).then_inc(s_xnw, 1)
                emit_stt3(2)
            emit_stt3(64)
            phase_end(v, 2)
            phase_wait1(v, 2)
            for h in range(W12):
                for tsub in range(TS):
                    gi = h * TS + tsub
                    v.wait_ge(s_h2, gi + 1)
                    v.wait_ge(s_sg, gi + 1)
                    if tsub == 0 and h >= 2:
                        pwait(v, s_hst, h - 2)
                    v.tensor_mul(out=hid_sb[h % 2][:, 512 * tsub:512 * (tsub + 1)],
                                 in0=bps[gi % 2][:], in1=sig[gi % 2][:]
                                 ).then_inc(s_hm, 1)
            phase_end(v, 3)
            phase_wait1(v, 3)
            for it in range(8 * TS):
                v.wait_ge(s_omm, it + 1)
                pwait(v, s_xl, it)
                if it >= 2:
                    pwait(v, s_ost, it - 2)
                v.tensor_add(out=osb[it % 2][:, 0:512], in0=ops_[0][:],
                             in1=xs[it % 2][:, 0:512])
                v.tensor_add(out=osb[it % 2][:, 512:1024], in0=ops_[1][:],
                             in1=xs[it % 2][:, 512:1024])
                v.drain()
                v.nop().then_inc(s_oev, 1)

        @blk.tensor
        def _(t_):
            t_.wait_ge(s_ini, 1)
            t_.wait_ge(s_gld, 16)
            t_.matmul(gpsA[:], ones_f[0:8, 0:1], gsb[:], start=True, stop=True
                      ).then_inc(s_g1, 1)
            t_.wait_ge(s_g3, 1)
            t_.matmul(nps[0][:, 0:4], ones_r[:], gtot[:], start=True, stop=True
                      ).then_inc(s_g4, 1)
            t_.wait_ge(s_g6, 1)
            phase_end(t_, 0)
            phase_wait1(t_, 0)
            for tt in range(TS):
                pwait(t_, s_qt, tt)
                if tt >= 2:
                    t_.wait_ge(s_den, tt - 1)
                for c in range(16):
                    mm = t_.matmul(dps[tt % 2][:], lhs_den[:, 64 * c:64 * (c + 1)],
                                   qt[tt % 2][:, c, :],
                                   start=(c == 0), stop=(c == 15))
                mm.then_inc(s_dmm, 1)
                t_.wait_ge(s_den, tt + 1)
                for c in range(16):
                    gi = tt * 16 + c
                    if gi >= 2:
                        t_.wait_ge(s_mxc, gi - 1)
                    t_.matmul(nps[gi % 2][:], lhs_num[:, c, :], qt[tt % 2][:, c, :],
                              start=True, stop=True).then_inc(s_nmm, 1)
                    if gi >= 2:
                        t_.wait_ge(s_bev, gi - 1)
                    t_.matmul(bps[gi % 2][:], ind_sb[:, 128 * c:128 * (c + 1)],
                              den_b[:], start=True, stop=True).then_inc(s_bmm, 1)
            phase_end(t_, 1)
            phase_wait1(t_, 1)
            phase_end(t_, 2)
            phase_wait1(t_, 2)
            t_.wait_ge(s_xnf, 256)
            for h in range(W12):
                pwait(t_, s_l1, h)
                pwait(t_, s_l2, h)
                for tsub in range(TS):
                    gi = h * TS + tsub
                    if gi >= 2:
                        t_.wait_ge(s_sg, gi - 1)
                    for kc in range(16):
                        mm = t_.matmul(nps[gi % 2][:], l1[h % 2][:, kc, :],
                                       xnf[:, T * kc + 512 * tsub:
                                           T * kc + 512 * (tsub + 1)],
                                       start=(kc == 0), stop=(kc == 15))
                    mm.then_inc(s_h1, 1)
                    if gi >= 2:
                        t_.wait_ge(s_hm, gi - 1)
                    for kc in range(16):
                        mm = t_.matmul(bps[gi % 2][:], l2[h % 2][:, kc, :],
                                       xnf[:, T * kc + 512 * tsub:
                                           T * kc + 512 * (tsub + 1)],
                                       start=(kc == 0), stop=(kc == 15))
                    mm.then_inc(s_h2, 1)
            phase_end(t_, 3)
            phase_wait1(t_, 3)
            t_.wait_ge(s_bpr, 16)
            for mh in range(2):
                t_.wait_ge(s_res, 16 * 60 * (mh + 1))
                for tt in range(TS):
                    gmt = mh * TS + tt
                    pwait(t_, s_mxt, gmt)
                    for su in range(4):
                        it = gmt * 4 + su
                        pwait(t_, s_hl, it, k=64)
                        if it >= 1:
                            t_.wait_ge(s_oev, it)
                        for q2 in range(2):
                            for c in range(16):
                                t_.matmul(ops_[q2][:],
                                          mixt[gmt % 2][:, c, 128 * su:128 * (su + 1)],
                                          wpr[:, c, 512 * q2:512 * (q2 + 1)],
                                          start=(c == 0), stop=False)
                            for hc in range(44):
                                t_.matmul(ops_[q2][:],
                                          hidl[it % 2][:, hc, :],
                                          w3r[:, hc, 512 * q2:512 * (q2 + 1)],
                                          start=False, stop=False)
                            mm = t_.matmul(ops_[q2][:], ones_bf[:],
                                           bpr[:, 1024 * mh + 512 * q2:
                                               1024 * mh + 512 * (q2 + 1)],
                                           start=False, stop=True)
                        mm.then_inc(s_omm, 1)
    ctx.close()
    return nc

def _mk_ind():
    ind = np.zeros((64, 2048), np.float32)
    for c in range(16):
        for j in range(4):
            ind[4 * c + j, 128 * c + 32 * j:128 * c + 32 * (j + 1)] = 1.0
    return ind.astype(BF)


def make_A_inmaps(x_flat, wqkv, w1, w2, w3, T, n_cores=8):
    in_maps = []
    for c in range(n_cores):
        gsh = np.stack([
            w1[704 * c:704 * (c + 1), :].reshape(128, GW),
            w2[704 * c:704 * (c + 1), :].reshape(128, GW),
            np.ascontiguousarray(w3[:, 704 * c:704 * (c + 1)]).reshape(128, GW),
        ])
        in_maps.append(dict(x=np.ascontiguousarray(x_flat[T * c:T * (c + 1)]),
                            wqkv=wqkv, gsh=np.ascontiguousarray(gsh)))
    return in_maps


def kernel(L0_attn_x, L0_attn_w_qkv, L0_attn_w_proj, L0_attn_b_proj,
           L0_ffn_w1, L0_ffn_w2, L0_ffn_w3):
    x = np.ascontiguousarray(np.asarray(L0_attn_x, np.float32).reshape(32768, D))
    wqkv = np.ascontiguousarray(np.asarray(L0_attn_w_qkv, np.float32))
    wproj = np.ascontiguousarray(np.asarray(L0_attn_w_proj, np.float32))
    bproj = np.ascontiguousarray(np.asarray(L0_attn_b_proj, np.float32).reshape(1, D))
    w1 = np.ascontiguousarray(np.asarray(L0_ffn_w1, np.float32))
    w2 = np.ascontiguousarray(np.asarray(L0_ffn_w2, np.float32))
    w3 = np.ascontiguousarray(np.asarray(L0_ffn_w3, np.float32))

    T = 4096
    ncA = build_A(T)
    resA = run_bass_kernel_spmd(ncA, make_A_inmaps(x, wqkv, w1, w2, w3, T),
                                core_ids=list(range(8)))
    vkp = np.ascontiguousarray(
        np.stack([np.asarray(resA.results[c]["vkT_out"]) for c in range(8)]))
    gsums = np.zeros((8, 4), np.float32)
    for c in range(8):
        gsums[c, 0:3] = np.asarray(resA.results[c]["gsum_out"])[0]
    ind = _mk_ind()
    ncB = build_B(T)
    in_maps = []
    for c in range(8):
        bselw = np.zeros((32, 8), np.float32)
        bselw[:, 4 * (c // 4):4 * (c // 4) + 4] = 1.0
        in_maps.append(dict(
            x=np.ascontiguousarray(x[T * c:T * (c + 1)]),
            q_in=np.ascontiguousarray(np.asarray(resA.results[c]["q_out"])),
            vkp=vkp, bselw=bselw, gsums=gsums, ind=ind,
            w1=w1, w2=w2, w3=w3, wproj=wproj, bproj=bproj))
    resB = run_bass_kernel_spmd(ncB, in_maps, core_ids=list(range(8)))
    outs = np.concatenate([np.asarray(resB.results[c]["out"]) for c in range(8)], axis=0)
    return outs.reshape(2, 16384, D)


# version 10
# speedup vs baseline: 1.0187x; 1.0187x over previous
import numpy as np
import ml_dtypes
from contextlib import ExitStack

import concourse.bass as bass
import concourse.mybir as mybir
from concourse.bass_utils import run_bass_kernel_spmd

F32 = mybir.dt.float32
BF16 = mybir.dt.bfloat16
AF = mybir.ActivationFunctionType
ALU = mybir.AluOpType
AX = mybir.AxisListType
BF = ml_dtypes.bfloat16

D = 2048
QKV = 3 * D
HID = 5632
NW = HID * D
GW = NW // (8 * 128)
GQ = GW // 8
EPS_LN = 1e-5
EPS_ATT = 1e-15
EPS_Q = 1e-8



def build_A(T):
    NT = T // 128
    TS = T // 512
    nc = bass.Bass("TRN2", target_bir_lowering=False)

    x = nc.dram_tensor("x", [T, D], F32, kind="ExternalInput").ap()
    wqkv = nc.dram_tensor("wqkv", [QKV, D], F32, kind="ExternalInput").ap()
    gsh = nc.dram_tensor("gsh", [3, 128, GW], F32, kind="ExternalInput").ap()

    q_out = nc.dram_tensor("q_out", [D, T], BF16, kind="ExternalOutput").ap()
    vkT_out = nc.dram_tensor("vkT_out", [32, 2112], F32, kind="ExternalOutput").ap()
    gsum_out = nc.dram_tensor("gsum_out", [1, 3], F32, kind="ExternalOutput").ap()

    xn_scr = nc.dram_tensor("xn_scr", [T, D], BF16, kind="Internal").ap()
    wqkv_bf = nc.dram_tensor("wqkv_bf", [QKV, D], BF16, kind="Internal").ap()
    wqkvT = nc.dram_tensor("wqkvT", [D, QKV], BF16, kind="Internal").ap()
    k_scr = nc.dram_tensor("k_scr", [D, T], BF16, kind="Internal").ap()
    v_scr = nc.dram_tensor("v_scr", [D, T], BF16, kind="Internal").ap()

    ctx = ExitStack()
    sb = lambda n, sh, dt: ctx.enter_context(nc.sbuf_tensor(n, sh, dt))
    ps = lambda n, sh: ctx.enter_context(nc.psum_tensor(n, sh, F32))
    sem = lambda n: ctx.enter_context(nc.semaphore(n))
    psem = lambda n: [sem(n + "0"), sem(n + "1")]

    def pwait(eng, sp, idx):
        eng.wait_ge(sp[idx % 2], 16 * (idx // 2 + 1))

    def pall(eng, sp, n):
        if (n + 1) // 2:
            eng.wait_ge(sp[0], 16 * ((n + 1) // 2))
        if n // 2:
            eng.wait_ge(sp[1], 16 * (n // 2))

    xnf = sb("xnf", [128, 16 * T], BF16)
    lhsT = [sb(f"lhsT{p}", [128, 16, 128], BF16) for p in range(2)]
    ev = [sb(f"ev{p}", [128, T], BF16) for p in range(2)]
    wstrip = [sb(f"wstrip{p}", [128, QKV // 2], BF16) for p in range(2)]
    ones_f = sb("ones_f", [128, 1], F32)
    gcols = sb("gcols", [128, 28], F32)
    gst = sb("gst", [128, GQ], F32)
    gev = sb("gev", [1, 3], F32)
    msum = sb("msum", [128, 2], F32)
    negm = sb("negm", [128, 2], F32)
    sqs = sb("sqs", [128, 2], F32)
    stdv = sb("stdv", [128, 2], F32)
    rstd = sb("rstd", [128, 2], F32)
    lnsc = sb("lnsc", [128, 1], F32)
    lneps = sb("lneps", [128, 1], F32)

    ph1 = ExitStack()
    sb1 = lambda n, sh, dt: ph1.enter_context(nc.sbuf_tensor(n, sh, dt))
    sx = [sb1(f"sx{p}", [128, D], F32) for p in range(2)]
    dump = sb1("dump", [128, D], BF16)
    xnb = [sb1(f"xnb{p}", [128, D], BF16) for p in range(2)]
    ph1.close()
    ktm = [sb(f"ktm{p}", [128, D], BF16) for p in range(2)]
    vtm = [sb(f"vtm{p}", [128, D], BF16) for p in range(2)]
    v33 = [sb(f"v33{p}", [128, 2112], BF16) for p in range(2)]
    vkev = sb("vkev", [128, 2112], F32)

    pp = [ps(f"pp{p}", [128, 512]) for p in range(2)]
    phg = ExitStack()
    gps = phg.enter_context(nc.psum_tensor("gps", [1, 3], F32))
    phg.close()
    vkps = ps("vkps", [128, 3072])

    def vk_slice(c):
        st = 512 * (c // 3) + 132 * (c % 3)
        return vkps[:, st:st + 132]

    s_gld = sem("s_gld"); s_gr = sem("s_gr"); s_gmm = sem("s_gmm"); s_gev = sem("s_gev")
    s_x = psem("s_x"); s_ctf = sem("s_ctf"); s_sq = sem("s_sq"); s_xnw = sem("s_xnw")
    s_xns = psem("s_xns"); s_xnf = sem("s_xnf")
    s_wb = sem("s_wb"); s_wts = psem("s_wts"); s_wtd = psem("s_wtd")
    s_lhs = psem("s_lhs"); s_mm = sem("s_mm"); s_mmc = sem("s_mmc"); s_evq = sem("s_evq")
    s_cst = psem("s_cst")
    s_kt = psem("s_kt"); s_vt = psem("s_vt"); s_v33 = sem("s_v33"); s_vkt = sem("s_vkt")
    s_vka = sem("s_vka"); s_vkst = sem("s_vkst"); s_gst = sem("s_gst")

    chunk_order = list(range(16, 48)) + list(range(0, 16))

    def chunk_dst(m):
        if m < 16:
            return q_out[128 * m:128 * (m + 1), :]
        if m < 32:
            return k_scr[128 * (m - 16):128 * (m - 15), :]
        return v_scr[128 * (m - 32):128 * (m - 31), :]

    with nc.Block() as blk:

        @blk.sync
        def _(sync):
            for i in range(NT):
                if i >= 2:
                    sync.wait_ge(s_xnw, i - 1)
                sync.dma_start(out=sx[i % 2][:], in_=x[128 * i:128 * (i + 1), :]
                               ).then_inc(s_x[i % 2], 16)
            for hs in range(32):
                kc, half = hs // 2, hs % 2
                pwait(sync, s_wts, hs)
                sync.dma_start(
                    out=wqkvT[128 * kc:128 * (kc + 1),
                              (QKV // 2) * half:(QKV // 2) * (half + 1)],
                    in_=wstrip[hs % 2][:]).then_inc(s_wtd[hs % 2], 16)
            pall(sync, s_wtd, 32)
            for j, m in enumerate(chunk_order):
                if j >= 2:
                    sync.wait_ge(s_mmc, j - 1)
                src = wqkvT[:, 128 * m:128 * (m + 1)].rearrange(
                    "(c p) m -> p c m", p=128)
                sync.dma_start(out=lhsT[j % 2][:], in_=src
                               ).then_inc(s_lhs[j % 2], 16)
                if j >= 1:
                    jj = j - 1
                    sync.wait_ge(s_evq, jj * TS + TS)
                    sync.dma_start(out=chunk_dst(chunk_order[jj]),
                                   in_=ev[jj % 2][:]).then_inc(s_cst[jj % 2], 16)
            jj = len(chunk_order) - 1
            sync.wait_ge(s_evq, jj * TS + TS)
            sync.dma_start(out=chunk_dst(chunk_order[jj]),
                           in_=ev[jj % 2][:]).then_inc(s_cst[jj % 2], 16)
            sync.wait_ge(s_gev, 1)
            sync.dma_start(out=gsum_out, in_=gev[:]).then_inc(s_gst, 16)
            sync.wait_ge(s_vka, 2 * NT)
            for jh in range(4):
                srcv = vkev[32 * jh:32 * (jh + 1), :].rearrange(
                    "d (c q) -> d c q", q=132)[:, :, 33 * jh:33 * (jh + 1)]
                dstv = vkT_out.rearrange(
                    "d (c q) -> d c q", q=132)[:, :, 33 * jh:33 * (jh + 1)]
                sync.dma_start(out=dstv, in_=srcv).then_inc(s_vkst, 16)
            pall(sync, s_cst, 48)
            sync.wait_ge(s_vkst, 64)
            sync.wait_ge(s_gst, 16)

        @blk.gpsimd
        def _(g):
            for cchunk in range(12):
                g.dma_start(out=wqkv_bf[512 * cchunk:512 * (cchunk + 1), :],
                            in_=wqkv[512 * cchunk:512 * (cchunk + 1), :]
                            ).then_inc(s_wb, 16)
            for i in range(NT):
                g.wait_ge(s_xnw, i + 1)
                g.dma_start(out=xn_scr[128 * i:128 * (i + 1), :],
                            in_=xnb[i % 2][:]).then_inc(s_xns[i % 2], 16)

        @blk.scalar
        def _(s):
            for i in range(NT):
                p = i % 2
                s.wait_ge(s_ctf, i + 1)
                s.activation(out=dump[:], in_=sx[p][:], func=AF.Square,
                             accum_out=sqs[:, p:p + 1])
                s.drain()
                s.activation(out=stdv[:, p:p + 1],
                             in_=sqs[:, p:p + 1], func=AF.Sqrt,
                             scale=lnsc[:, 0:1], bias=lneps[:, 0:1]
                             ).then_inc(s_sq, 1)
            s.wait_ge(s_wb, 16 * 12)
            for hs in range(32):
                kc, half = hs // 2, hs % 2
                if hs >= 2:
                    pwait(s, s_wtd, hs - 2)
                s.dma_start(out=wstrip[hs % 2][:],
                            in_=wqkv_bf[(QKV // 2) * half:(QKV // 2) * (half + 1),
                                        128 * kc:128 * (kc + 1)],
                            transpose=True).then_inc(s_wts[hs % 2], 16)
            pall(s, s_xns, NT)
            for c in range(16):
                s.dma_start(out=xnf[:, T * c:T * (c + 1)],
                            in_=xn_scr[:, 128 * c:128 * (c + 1)],
                            transpose=True).then_inc(s_xnf, 16)
            for j, m in enumerate(chunk_order):
                func = AF.Copy if 32 <= m < 48 else AF.Relu
                for tsub in range(TS):
                    gidx = j * TS + tsub
                    if j >= 2 and tsub == 0:
                        pwait(s, s_cst, j - 2)
                    s.wait_ge(s_mm, gidx + 1)
                    s.activation(out=ev[j % 2][:, 512 * tsub:512 * (tsub + 1)],
                                 in_=pp[gidx % 2][:], func=func
                                 ).then_inc(s_evq, 1)
            pall(s, s_cst, 32)
            for tv in range(NT):
                if tv >= 2:
                    s.wait_ge(s_vkt, tv - 1)
                s.dma_start(out=ktm[tv % 2][:],
                            in_=k_scr[:, 128 * tv:128 * (tv + 1)],
                            transpose=True).then_inc(s_kt[tv % 2], 16)
                s.dma_start(out=vtm[tv % 2][:],
                            in_=v_scr[:, 128 * tv:128 * (tv + 1)],
                            transpose=True).then_inc(s_vt[tv % 2], 16)
            for gq in range(24):
                w_i, qq = gq // 8, gq % 8
                if gq >= 1:
                    s.wait_ge(s_gr, gq)
                s.dma_start(out=gst[:],
                            in_=gsh[w_i, :, GQ * qq:GQ * (qq + 1)]
                            ).then_inc(s_gld, 16)

        @blk.vector
        def _(v):
            v.memset(ones_f[:], 1.0)
            v.memset(lnsc[:], 1.0 / D)
            v.memset(lneps[:], EPS_LN)
            v.drain()
            for i in range(NT):
                p = i % 2
                pwait(v, s_x, i)
                if i >= 2:
                    v.wait_ge(s_sq, i - 1)
                    pwait(v, s_xns, i - 2)
                v.reduce_sum(out=msum[:, p:p + 1], in_=sx[p][:], axis=AX.X)
                v.drain()
                v.tensor_scalar_mul(out=negm[:, p:p + 1], in0=msum[:, p:p + 1],
                                    scalar1=-1.0 / D)
                v.drain()
                v.tensor_scalar_add(out=sx[p][:], in0=sx[p][:],
                                    scalar1=negm[:, p:p + 1]).then_inc(s_ctf, 1)
                v.wait_ge(s_sq, i + 1)
                v.reciprocal(out=rstd[:, p:p + 1], in_=stdv[:, p:p + 1])
                v.drain()
                v.tensor_scalar_mul(out=xnb[p][:], in0=sx[p][:],
                                    scalar1=rstd[:, p:p + 1]).then_inc(s_xnw, 1)
            pall(v, s_xns, NT)
            for p in range(2):
                v.memset(v33[p][:].rearrange("p (h e) -> p h e", e=33)[:, :, 32:33], 1.0)
            v.memset(vkev[:], 0.0)
            v.drain()
            for t in range(NT):
                pwait(v, s_vt, t)
                if t >= 2:
                    v.wait_ge(s_vkt, t - 1)
                v.tensor_copy(
                    out=v33[t % 2][:].rearrange("p (h e) -> p h e", e=33)[:, :, 0:32],
                    in_=vtm[t % 2][:].rearrange("p (h e) -> p h e", e=32),
                ).then_inc(s_v33, 1)
                v.wait_ge(s_vkt, t + 1)
                v.drain()
                v.tensor_add(
                    out=vkev[:, 0:1980].rearrange("p (b g) -> p b g", g=396),
                    in0=vkps[:].rearrange("p (b g) -> p b g", g=512)[:, 0:5, 0:396],
                    in1=vkev[:, 0:1980].rearrange("p (b g) -> p b g", g=396),
                ).then_inc(s_vka, 1)
                v.tensor_add(out=vkev[:, 1980:2112], in0=vkps[:, 2560:2692],
                             in1=vkev[:, 1980:2112]).then_inc(s_vka, 1)
            for gq in range(24):
                w_i = gq // 8
                v.wait_ge(s_gld, 16 * (gq + 1))
                v.reduce_sum(out=gcols[:, gq:gq + 1], in_=gst[:],
                             axis=AX.X, apply_absolute_value=True
                             ).then_inc(s_gr, 1)
            v.wait_ge(s_gr, 24)
            for w_i in range(3):
                v.reduce_sum(out=gcols[:, 24 + w_i:25 + w_i],
                             in_=gcols[:, 8 * w_i:8 * (w_i + 1)], axis=AX.X
                             ).then_inc(s_gr, 1)
            v.wait_ge(s_gmm, 1)
            v.tensor_copy(out=gev[:], in_=gps[:]).then_inc(s_gev, 1)

        @blk.tensor
        def _(t_):
            t_.wait_ge(s_xnf, 256)
            for j, m in enumerate(chunk_order):
                pwait(t_, s_lhs, j)
                for tsub in range(TS):
                    gidx = j * TS + tsub
                    if gidx >= 2:
                        t_.wait_ge(s_evq, gidx - 1)
                    for kc in range(16):
                        mm = t_.matmul(pp[gidx % 2][:],
                                       lhsT[j % 2][:, kc, :],
                                       xnf[:, T * kc + 512 * tsub:
                                           T * kc + 512 * (tsub + 1)],
                                       start=(kc == 0), stop=(kc == 15))
                    mm.then_inc(s_mm, 1)
                    if tsub == TS - 1:
                        t_.nop().then_inc(s_mmc, 1)
            for tv in range(NT):
                pwait(t_, s_kt, tv)
                t_.wait_ge(s_v33, tv + 1)
                if tv >= 1:
                    t_.wait_ge(s_vka, 2 * tv)
                for c in range(16):
                    mm = t_.matmul(vk_slice(c),
                                   ktm[tv % 2][:, 128 * c:128 * (c + 1)],
                                   v33[tv % 2][:, 132 * c:132 * (c + 1)],
                                   start=True, stop=True)
                mm.then_inc(s_vkt, 1)
            t_.wait_ge(s_vka, 2 * NT)
            t_.wait_ge(s_gr, 27)
            t_.matmul(gps[:], ones_f[:], gcols[:, 24:27], start=True, stop=True
                      ).then_inc(s_gmm, 1)

    ctx.close()
    return nc



def build_B(T):
    NT = T // 128
    TS = T // 512
    W12 = HID // 128
    nc = bass.Bass("TRN2", target_bir_lowering=False)

    x = nc.dram_tensor("x", [T, D], F32, kind="ExternalInput").ap()
    q_in = nc.dram_tensor("q_in", [D, T], BF16, kind="ExternalInput").ap()
    vkp = nc.dram_tensor("vkp", [8, 32, 2112], F32, kind="ExternalInput").ap()
    bselw = nc.dram_tensor("bselw", [32, 8], F32, kind="ExternalInput").ap()
    gsums = nc.dram_tensor("gsums", [8, 4], F32, kind="ExternalInput").ap()
    ind = nc.dram_tensor("ind", [64, 2048], BF16, kind="ExternalInput").ap()
    w1 = nc.dram_tensor("w1", [HID, D], F32, kind="ExternalInput").ap()
    w2 = nc.dram_tensor("w2", [HID, D], F32, kind="ExternalInput").ap()
    w3 = nc.dram_tensor("w3", [D, HID], F32, kind="ExternalInput").ap()
    wproj = nc.dram_tensor("wproj", [D, D], F32, kind="ExternalInput").ap()
    bproj = nc.dram_tensor("bproj", [1, D], F32, kind="ExternalInput").ap()

    out = nc.dram_tensor("out", [T, D], F32, kind="ExternalOutput").ap()

    vk_dram = nc.dram_tensor("vk_dram", [32, 2112], F32, kind="Internal").ap()
    mix_scr = nc.dram_tensor("mix_scr", [D, T], BF16, kind="Internal").ap()
    xn_scr = nc.dram_tensor("xn_scr", [T, D], BF16, kind="Internal").ap()
    hid_scr = nc.dram_tensor("hid_scr", [HID, T], BF16, kind="Internal").ap()
    w1q_bf = nc.dram_tensor("w1q_bf", [HID, D], BF16, kind="Internal").ap()
    w2q_bf = nc.dram_tensor("w2q_bf", [HID, D], BF16, kind="Internal").ap()
    w3q_bf = nc.dram_tensor("w3q_bf", [D, HID], BF16, kind="Internal").ap()
    wproj_bf = nc.dram_tensor("wproj_bf", [D, D], BF16, kind="Internal").ap()
    w1qT = nc.dram_tensor("w1qT", [D, HID], BF16, kind="Internal").ap()
    w2qT = nc.dram_tensor("w2qT", [D, HID], BF16, kind="Internal").ap()
    w3qT = nc.dram_tensor("w3qT", [HID, D], BF16, kind="Internal").ap()
    wprojT = nc.dram_tensor("wprojT", [D, D], BF16, kind="Internal").ap()

    ctx = ExitStack()
    sb = lambda n, sh, dt: ctx.enter_context(nc.sbuf_tensor(n, sh, dt))
    ps = lambda n, sh: ctx.enter_context(nc.psum_tensor(n, sh, F32))
    sem = lambda n: ctx.enter_context(nc.semaphore(n))
    psem = lambda n: [sem(n + "0"), sem(n + "1")]

    def pwait(eng, sp, idx, k=16):
        eng.wait_ge(sp[idx % 2], k * (idx // 2 + 1))

    def pall(eng, sp, n, k=16):
        if (n + 1) // 2:
            eng.wait_ge(sp[0], k * ((n + 1) // 2))
        if n // 2:
            eng.wait_ge(sp[1], k * (n // 2))

    ones_f = sb("ones_f", [128, 1], F32)
    ones_r = sb("ones_r", [1, 128], F32)
    ones_bf = sb("ones_bf", [1, 128], BF16)
    gsb = sb("gsb", [8, 4], F32)
    gtot = sb("gtot", [1, 4], F32)
    thr = sb("thr", [128, 4], F32)
    nthr = sb("nthr", [128, 4], F32)
    ind_sb = sb("ind_sb", [64, 2048], BF16)
    bpr = sb("bpr", [1, D], BF16)
    bselw_sb = sb("bselw_sb", [32, 8], F32)
    lhs_num = sb("lhs_num", [128, 16, 128], BF16)
    lhs_den = sb("lhs_den", [128, 1024], BF16)
    vkacc = sb("vkacc", [32, 2112], F32)

    ph01 = ExitStack()
    sA = lambda n, sh, dt: ph01.enter_context(nc.sbuf_tensor(n, sh, dt))
    vkt_b = [sA(f"vkt{p}", [32, 2112], F32) for p in range(2)]
    qt = [sA(f"qt{p}", [128, 16, 512], BF16) for p in range(2)]
    mix_sb = [sA(f"mix_sb{p}", [128, 16, 512], BF16) for p in range(2)]
    den_f = sA("den_f", [64, 512], F32)
    den_r = sA("den_r", [64, 512], F32)
    den_b = sA("den_b", [64, 512], BF16)
    bc_sb = [sA(f"bc{p}", [128, 512], BF16) for p in range(2)]
    wfA = [sA(f"wfA{p}", [128, 2048], F32) for p in range(2)]
    wltA = sA("wltA", [128, 2048], BF16)
    wqA = [sA(f"wqA{p}", [128, 2048], BF16) for p in range(2)]
    wsA = [sA(f"wsA{p}", [128, HID // 2], BF16) for p in range(2)]
    ph01.close()

    phx = ExitStack()
    xnf = phx.enter_context(nc.sbuf_tensor("xnf", [128, 16 * T], BF16))

    ph2 = ExitStack()
    sB = lambda n, sh, dt: ph2.enter_context(nc.sbuf_tensor(n, sh, dt))
    sx = [sB(f"sx{p}", [128, D], F32) for p in range(2)]
    dump = sB("dump", [128, D], BF16)
    xnb = [sB(f"xnb{p}", [128, D], BF16) for p in range(2)]
    msum = sB("msum", [128, 2], F32)
    negm = sB("negm", [128, 2], F32)
    sqs = sB("sqs", [128, 2], F32)
    stdv = sB("stdv", [128, 2], F32)
    rstd = sB("rstd", [128, 2], F32)
    lnsc = sB("lnsc", [128, 1], F32)
    lneps = sB("lneps", [128, 1], F32)
    wfB = [sB(f"wfB{p}", [128, 1408], F32) for p in range(2)]
    wltB = sB("wltB", [128, 1408], BF16)
    wqB = [sB(f"wqB{p}", [128, 1408], BF16) for p in range(2)]
    wsB = [sB(f"wsB{p}", [128, 2048], BF16) for p in range(2)]
    ph2.close()

    ph3 = ExitStack()
    sC = lambda n, sh, dt: ph3.enter_context(nc.sbuf_tensor(n, sh, dt))
    l1 = [sC(f"l1_{p}", [128, 16, 128], BF16) for p in range(2)]
    l2 = [sC(f"l2_{p}", [128, 16, 128], BF16) for p in range(2)]
    sig = [sC(f"sig{p}", [128, 512], BF16) for p in range(2)]
    hid_sb = [sC(f"hid{p}", [128, T], BF16) for p in range(2)]
    ph3.close()
    phx.close()

    ph4 = ExitStack()
    sD = lambda n, sh, dt: ph4.enter_context(nc.sbuf_tensor(n, sh, dt))
    w3r = sD("w3r", [128, 44, 1024], BF16)
    wpr = sD("wpr", [128, 16, 1024], BF16)
    mixt = [sD("mixt0", [128, 16, 512], BF16)] * 2
    hidl = [sD(f"hidl{p}", [128, 44, 128], BF16) for p in range(2)]
    xs = [sD(f"xs{p}", [128, 1024], F32) for p in range(2)]
    osb = [sD(f"osb{p}", [128, 1024], F32) for p in range(2)]
    ph4.close()

    phg = ExitStack()
    gpsA = phg.enter_context(nc.psum_tensor("gpsA", [1, 4], F32))
    phg.close()
    dps = [ps(f"dps{p}", [64, 512]) for p in range(2)]
    nps = [ps(f"nps{p}", [128, 512]) for p in range(2)]
    bps = [ps(f"bps{p}", [128, 512]) for p in range(2)]
    ops_ = [ps(f"ops{p}", [128, 512]) for p in range(2)]

    s_ph = [sem(f"s_ph{k}") for k in range(5)]
    s_ini = sem("s_ini")
    s_gld = sem("s_gld"); s_ind = sem("s_ind"); s_bsl = sem("s_bsl")
    s_bpr = sem("s_bpr")
    s_g1 = sem("s_g1"); s_g2 = sem("s_g2"); s_g3 = sem("s_g3")
    s_g4 = sem("s_g4"); s_g5 = sem("s_g5"); s_g6 = sem("s_g6")
    s_vl = psem("s_vl"); s_vac = sem("s_vac"); s_vkd = sem("s_vkd")
    s_lnb = sem("s_lnb"); s_ldz = sem("s_ldz")
    s_qt = psem("s_qt"); s_dmm = sem("s_dmm"); s_den = sem("s_den")
    s_nmm = sem("s_nmm"); s_bmm = sem("s_bmm"); s_bev = sem("s_bev")
    s_mxc = sem("s_mxc"); s_mxs = psem("s_mxs")
    s_wld = psem("s_wld"); s_wq = sem("s_wq")
    s_wqs = psem("s_wqs"); s_wts = psem("s_wts"); s_wtd = psem("s_wtd")
    s_x = psem("s_x"); s_ctf = sem("s_ctf"); s_sq = sem("s_sq"); s_xnw = sem("s_xnw")
    s_xns = psem("s_xns"); s_xnf = sem("s_xnf")
    s_w3ld = psem("s_w3ld"); s_w3q = sem("s_w3q")
    s_w3qs = psem("s_w3qs"); s_w3ts = psem("s_w3ts"); s_w3td = psem("s_w3td")
    s_wpc = sem("s_wpc"); s_wpts = psem("s_wpts"); s_wptd = psem("s_wptd")
    s_l1 = psem("s_l1"); s_l2 = psem("s_l2")
    s_h1 = sem("s_h1"); s_h2 = sem("s_h2"); s_sg = sem("s_sg"); s_hm = sem("s_hm")
    s_hst = psem("s_hst")
    s_res = sem("s_res")
    s_mxt = psem("s_mxt"); s_hl = psem("s_hl"); s_xl = psem("s_xl")
    s_omm = sem("s_omm"); s_oev = sem("s_oev"); s_ost = psem("s_ost")

    NENG = 5

    def phase_end(eng, k):
        eng.drain()
        eng.nop().then_inc(s_ph[k], 1)

    def phase_wait(eng, k):
        eng.wait_ge(s_ph[k], NENG * (k + 1))

    def phase_wait1(eng, k):
        eng.wait_ge(s_ph[k], NENG)

    with nc.Block() as blk:

        @blk.sync
        def _(sync):
            sync.dma_start(out=gsb[:], in_=gsums).then_inc(s_gld, 16)
            sync.dma_start(out=ind_sb[:], in_=ind).then_inc(s_ind, 16)
            sync.dma_start(out=bselw_sb[:], in_=bselw).then_inc(s_bsl, 16)
            for r in range(8):
                if r >= 2:
                    sync.wait_ge(s_vac, r - 1)
                sync.dma_start(out=vkt_b[r % 2][:], in_=vkp[r, :, :]
                               ).then_inc(s_vl[r % 2], 16)
            sync.wait_ge(s_vac, 8)
            sync.dma_start(out=vk_dram, in_=vkacc[:]).then_inc(s_vkd, 16)
            sync.wait_ge(s_vkd, 16)
            sync.wait_ge(s_gld, 16)
            sync.wait_ge(s_ind, 16)
            sync.wait_ge(s_bsl, 16)
            phase_end(sync, 0)
            phase_wait1(sync, 0)
            for tt in range(TS):
                if tt >= 2:
                    sync.wait_ge(s_nmm, 16 * (tt - 1))
                sync.dma_start(
                    out=qt[tt % 2][:],
                    in_=q_in[:, 512 * tt:512 * (tt + 1)].rearrange(
                        "(c p) n -> p c n", p=128)).then_inc(s_qt[tt % 2], 16)
                if tt >= 1:
                    sync.wait_ge(s_mxc, 16 * tt)
                    sync.dma_start(
                        out=mix_scr[:, 512 * (tt - 1):512 * tt].rearrange(
                            "(c p) n -> p c n", p=128),
                        in_=mix_sb[(tt - 1) % 2][:]).then_inc(s_mxs[(tt - 1) % 2], 16)
            sync.wait_ge(s_mxc, 16 * TS)
            sync.dma_start(
                out=mix_scr[:, 512 * (TS - 1):512 * TS].rearrange(
                    "(c p) n -> p c n", p=128),
                in_=mix_sb[(TS - 1) % 2][:]).then_inc(s_mxs[(TS - 1) % 2], 16)
            for g in range(2 * 32):
                w_i, hs = g // 32, g % 32
                kc, half = hs // 2, hs % 2
                wTd = w1qT if w_i == 0 else w2qT
                pwait(sync, s_wts, g)
                sync.dma_start(
                    out=wTd[128 * kc:128 * (kc + 1),
                            (HID // 2) * half:(HID // 2) * (half + 1)],
                    in_=wsA[g % 2][:]).then_inc(s_wtd[g % 2], 16)
            pall(sync, s_mxs, TS)
            pall(sync, s_wtd, 64)
            phase_end(sync, 1)
            phase_wait1(sync, 1)
            for i in range(NT):
                if i >= 2:
                    sync.wait_ge(s_xnw, i - 1)
                sync.dma_start(out=sx[i % 2][:], in_=x[128 * i:128 * (i + 1), :]
                               ).then_inc(s_x[i % 2], 16)
            for g in range(44):
                pwait(sync, s_w3ts, g)
                sync.dma_start(out=w3qT[128 * g:128 * (g + 1), :],
                               in_=wsB[g % 2][:]).then_inc(s_w3td[g % 2], 16)
            for g in range(16):
                pwait(sync, s_wpts, g)
                sync.dma_start(out=wprojT[128 * g:128 * (g + 1), :],
                               in_=wsB[g % 2][:]).then_inc(s_wptd[g % 2], 16)
            pall(sync, s_xns, NT)
            pall(sync, s_w3td, 44)
            pall(sync, s_wptd, 16)
            phase_end(sync, 2)
            phase_wait1(sync, 2)
            for h in range(W12):
                if h >= 2:
                    sync.wait_ge(s_h2, TS * (h - 1))
                sync.dma_start(
                    out=l1[h % 2][:],
                    in_=w1qT[:, 128 * h:128 * (h + 1)].rearrange(
                        "(c p) m -> p c m", p=128)).then_inc(s_l1[h % 2], 16)
                sync.dma_start(
                    out=l2[h % 2][:],
                    in_=w2qT[:, 128 * h:128 * (h + 1)].rearrange(
                        "(c p) m -> p c m", p=128)).then_inc(s_l2[h % 2], 16)
                if h >= 1:
                    hh = h - 1
                    sync.wait_ge(s_hm, TS * (hh + 1))
                    sync.dma_start(out=hid_scr[128 * hh:128 * (hh + 1), :],
                                   in_=hid_sb[hh % 2][:]).then_inc(s_hst[hh % 2], 16)
            hh = W12 - 1
            sync.wait_ge(s_hm, TS * (hh + 1))
            sync.dma_start(out=hid_scr[128 * hh:128 * (hh + 1), :],
                           in_=hid_sb[hh % 2][:]).then_inc(s_hst[hh % 2], 16)
            pall(sync, s_hst, W12)
            phase_end(sync, 3)
            phase_wait1(sync, 3)
            for mh in range(2):
                if mh == 1:
                    sync.wait_ge(s_omm, 4 * TS)
                for g in range(44):
                    sync.dma_start(
                        out=w3r[:, g, :],
                        in_=w3qT[128 * g:128 * (g + 1),
                                 1024 * mh:1024 * (mh + 1)]).then_inc(s_res, 16)
                for g in range(16):
                    sync.dma_start(
                        out=wpr[:, g, :],
                        in_=wprojT[128 * g:128 * (g + 1),
                                   1024 * mh:1024 * (mh + 1)]).then_inc(s_res, 16)
                for tt in range(TS):
                    gmt = mh * TS + tt
                    if gmt >= 1:
                        sync.wait_ge(s_omm, 4 * gmt)
                    sync.dma_start(
                        out=mixt[gmt % 2][:],
                        in_=mix_scr[:, 512 * tt:512 * (tt + 1)].rearrange(
                            "(c p) n -> p c n", p=128)).then_inc(s_mxt[gmt % 2], 16)
                    for su in range(4):
                        it = gmt * 4 + su
                        tok = tt * 4 + su
                        if it >= 2:
                            sync.wait_ge(s_omm, it - 1)
                        srch = hid_scr[:, 128 * tok:128 * (tok + 1)].rearrange(
                            "(c p) n -> p c n", p=128)
                        for part in range(4):
                            sync.dma_start(out=hidl[it % 2][:, 11 * part:11 * (part + 1), :],
                                           in_=srch[:, 11 * part:11 * (part + 1), :]
                                           ).then_inc(s_hl[it % 2], 16)
                        if it >= 2:
                            sync.wait_ge(s_oev, it - 1)
                        sync.dma_start(out=xs[it % 2][:],
                                       in_=x[128 * tok:128 * (tok + 1),
                                             1024 * mh:1024 * (mh + 1)]
                                       ).then_inc(s_xl[it % 2], 16)
                        if it >= 1:
                            ip = it - 1
                            mhp = ip // (4 * TS)
                            tokp = ip % (4 * TS)
                            sync.wait_ge(s_oev, ip + 1)
                            sync.dma_start(
                                out=out[128 * tokp:128 * (tokp + 1),
                                        1024 * mhp:1024 * (mhp + 1)],
                                in_=osb[ip % 2][:]).then_inc(s_ost[ip % 2], 16)
            ip = 8 * TS - 1
            sync.wait_ge(s_oev, ip + 1)
            sync.dma_start(out=out[128 * (4 * TS - 1):128 * 4 * TS, 1024:2048],
                           in_=osb[ip % 2][:]).then_inc(s_ost[ip % 2], 16)
            pall(sync, s_ost, 8 * TS)

        @blk.gpsimd
        def _(g):
            g.dma_start(out=bpr[:], in_=bproj).then_inc(s_bpr, 16)
            g.wait_ge(s_vkd, 16)
            g.wait_ge(s_ldz, 1)
            for jh in range(4):
                srcn = vk_dram.rearrange("d (c q) -> d c q", q=132)[
                    :, :, 33 * jh:33 * jh + 32]
                g.dma_start(out=lhs_num[32 * jh:32 * (jh + 1), :,
                                         32 * jh:32 * (jh + 1)], in_=srcn
                            ).then_inc(s_lnb, 16)
                srcd = vk_dram[:, 33 * jh + 32::132][:, 0:16]
                flat = lhs_den[32 * jh:32 * (jh + 1), jh:1024:68][:, 0:16]
                g.dma_start(out=flat, in_=srcd,
                            allow_slow_non_contiguous=True).then_inc(s_lnb, 16)
            g.wait_ge(s_lnb, 16 * 8)
            g.wait_ge(s_bpr, 16)
            phase_end(g, 0)
            phase_wait1(g, 0)
            for k in range(2 * W12):
                hc = k % W12
                if k >= 2:
                    g.wait_ge(s_wq, k - 1)
                g.dma_start(out=wfA[k % 2][:],
                            in_=(w1 if k < W12 else w2)[128 * hc:128 * (hc + 1), :]
                            ).then_inc(s_wld[k % 2], 16)
                if k >= 1:
                    kk = k - 1
                    wdst = w1q_bf if kk < W12 else w2q_bf
                    hk = kk % W12
                    g.wait_ge(s_wq, kk + 1)
                    g.dma_start(out=wdst[128 * hk:128 * (hk + 1), :],
                                in_=wqA[kk % 2][:]).then_inc(s_wqs[kk % 2], 16)
            kk = 2 * W12 - 1
            g.wait_ge(s_wq, kk + 1)
            g.dma_start(out=w2q_bf[128 * (W12 - 1):128 * W12, :],
                        in_=wqA[kk % 2][:]).then_inc(s_wqs[kk % 2], 16)
            phase_end(g, 1)
            phase_wait1(g, 1)
            g.dma_start(out=wproj_bf, in_=wproj).then_inc(s_wpc, 16)
            for k3 in range(64):
                if k3 >= 2:
                    g.wait_ge(s_w3q, k3 - 1)
                g.dma_start(
                    out=wfB[k3 % 2][:],
                    in_=w3[128 * (k3 // 4):128 * (k3 // 4 + 1),
                           1408 * (k3 % 4):1408 * (k3 % 4 + 1)]
                    ).then_inc(s_w3ld[k3 % 2], 16)
                if k3 >= 1:
                    kk3 = k3 - 1
                    g.wait_ge(s_w3q, kk3 + 1)
                    g.dma_start(
                        out=w3q_bf[128 * (kk3 // 4):128 * (kk3 // 4 + 1),
                                   1408 * (kk3 % 4):1408 * (kk3 % 4 + 1)],
                        in_=wqB[kk3 % 2][:]).then_inc(s_w3qs[kk3 % 2], 16)
            kk3 = 63
            g.wait_ge(s_w3q, 64)
            g.dma_start(out=w3q_bf[128 * 15:128 * 16, 1408 * 3:5632],
                        in_=wqB[kk3 % 2][:]).then_inc(s_w3qs[kk3 % 2], 16)
            g.wait_ge(s_wpc, 16)
            phase_end(g, 2)
            phase_wait1(g, 2)
            phase_end(g, 3)
            phase_wait1(g, 3)

        @blk.scalar
        def _(s):
            s.wait_ge(s_g1, 1)
            s.activation(out=gtot[:], in_=gpsA[:], func=AF.Copy).then_inc(s_g2, 1)
            s.wait_ge(s_g4, 1)
            s.activation(out=thr[:], in_=nps[0][:, 0:4], func=AF.Copy
                         ).then_inc(s_g5, 1)
            phase_end(s, 0)
            phase_wait1(s, 0)
            for tt in range(TS):
                for c in range(16):
                    gi = tt * 16 + c
                    s.wait_ge(s_bmm, gi + 1)
                    if gi >= 2:
                        s.wait_ge(s_mxc, gi - 1)
                    s.activation(out=bc_sb[gi % 2][:], in_=bps[gi % 2][:],
                                 func=AF.Copy).then_inc(s_bev, 1)
            for g in range(2 * 32):
                w_i, hs = g // 32, g % 32
                kc, half = hs // 2, hs % 2
                wqb = w1q_bf if w_i == 0 else w2q_bf
                if hs == 0:
                    pall(s, s_wqs, W12 * (w_i + 1))
                if g >= 2:
                    pwait(s, s_wtd, g - 2)
                s.dma_start(out=wsA[g % 2][:],
                            in_=wqb[(HID // 2) * half:(HID // 2) * (half + 1),
                                    128 * kc:128 * (kc + 1)],
                            transpose=True).then_inc(s_wts[g % 2], 16)
            phase_end(s, 1)
            phase_wait1(s, 1)
            for i in range(NT):
                p = i % 2
                s.wait_ge(s_ctf, i + 1)
                s.activation(out=dump[:], in_=sx[p][:], func=AF.Square,
                             accum_out=sqs[:, p:p + 1])
                s.drain()
                s.activation(out=stdv[:, p:p + 1], in_=sqs[:, p:p + 1],
                             func=AF.Sqrt, scale=lnsc[:, 0:1],
                             bias=lneps[:, 0:1]).then_inc(s_sq, 1)
                s.wait_ge(s_xnw, i + 1)
                s.dma_start(out=xn_scr[128 * i:128 * (i + 1), :],
                            in_=xnb[i % 2][:]).then_inc(s_xns[i % 2], 16)
            pall(s, s_xns, NT)
            for c in range(16):
                s.dma_start(out=xnf[:, T * c:T * (c + 1)],
                            in_=xn_scr[:, 128 * c:128 * (c + 1)],
                            transpose=True).then_inc(s_xnf, 16)
            pall(s, s_w3qs, 64)
            for g in range(44):
                if g >= 2:
                    pwait(s, s_w3td, g - 2)
                s.dma_start(out=wsB[g % 2][:],
                            in_=w3q_bf[:, 128 * g:128 * (g + 1)],
                            transpose=True).then_inc(s_w3ts[g % 2], 16)
            s.wait_ge(s_wpc, 16)
            pwait(s, s_w3td, 42)
            pwait(s, s_w3td, 43)
            for g in range(16):
                if g >= 2:
                    pwait(s, s_wptd, g - 2)
                s.dma_start(out=wsB[g % 2][:],
                            in_=wproj_bf[:, 128 * g:128 * (g + 1)],
                            transpose=True).then_inc(s_wpts[g % 2], 16)
            phase_end(s, 2)
            phase_wait1(s, 2)
            for h in range(W12):
                for tsub in range(TS):
                    gi = h * TS + tsub
                    s.wait_ge(s_h1, gi + 1)
                    if gi >= 2:
                        s.wait_ge(s_hm, gi - 1)
                    s.activation(out=sig[gi % 2][:], in_=nps[gi % 2][:],
                                 func=AF.Sigmoid).then_inc(s_sg, 1)
            phase_end(s, 3)
            phase_wait1(s, 3)

        @blk.vector
        def _(v):
            v.memset(ones_f[:], 1.0)
            v.memset(ones_r[:], 1.0)
            v.memset(ones_bf[:], 1.0)
            v.memset(vkacc[:], 0.0)
            v.memset(lhs_den[:], 0.0)
            v.memset(lhs_num[:], 0.0)
            v.drain()
            v.nop().then_inc(s_ini, 1)
            v.nop().then_inc(s_ldz, 1)
            v.wait_ge(s_bsl, 16)
            for r in range(8):
                pwait(v, s_vl, r)
                v.drain()
                v.scalar_tensor_tensor(out=vkacc[:], in0=vkt_b[r % 2][:],
                                       scalar=bselw_sb[:, r:r + 1],
                                       in1=vkacc[:], op0=ALU.mult,
                                       op1=ALU.add).then_inc(s_vac, 1)
            v.wait_ge(s_g2, 1)
            v.tensor_scalar(out=gtot[:, 0:3], in0=gtot[:, 0:3],
                            scalar1=0.5 / NW, scalar2=0.5 * EPS_Q,
                            op0=ALU.mult, op1=ALU.add).then_inc(s_g3, 1)
            v.wait_ge(s_g5, 1)
            v.tensor_scalar_mul(out=nthr[:], in0=thr[:], scalar1=-1.0
                                ).then_inc(s_g6, 1)
            phase_end(v, 0)
            phase_wait1(v, 0)
            v_st = [0]

            def emit_stt(budget):
                for _ in range(budget):
                    k = v_st[0]
                    if k >= 2 * W12:
                        return
                    pwait(v, s_wld, k)
                    if k >= 2:
                        pwait(v, s_wqs, k - 2)
                    v.drain()
                    v.tensor_scalar(out=wltA[:], in0=wfA[k % 2][:],
                                    scalar1=nthr[:, k // W12:k // W12 + 1],
                                    scalar2=None, op0=ALU.is_lt)
                    v.drain()
                    v.scalar_tensor_tensor(
                        out=wqA[k % 2][:], in0=wfA[k % 2][:],
                        scalar=thr[:, k // W12:k // W12 + 1],
                        in1=wltA[:], op0=ALU.is_gt, op1=ALU.subtract
                        ).then_inc(s_wq, 1)
                    v_st[0] += 1

            for tt in range(TS):
                v.wait_ge(s_dmm, tt + 1)
                if tt >= 1:
                    v.wait_ge(s_bmm, 16 * tt)
                v.drain()
                v.tensor_scalar(out=den_f[:], in0=dps[tt % 2][:], scalar1=EPS_ATT,
                                scalar2=None, op0=ALU.add)
                v.drain()
                v.reciprocal(out=den_r[:], in_=den_f[:])
                v.drain()
                v.tensor_copy(out=den_b[:], in_=den_r[:]).then_inc(s_den, 1)
                for c in range(16):
                    gi = tt * 16 + c
                    v.wait_ge(s_nmm, gi + 1)
                    v.wait_ge(s_bev, gi + 1)
                    if c == 0 and tt >= 2:
                        pwait(v, s_mxs, tt - 2)
                    v.tensor_mul(out=mix_sb[tt % 2][:, c, :], in0=nps[gi % 2][:],
                                 in1=bc_sb[gi % 2][:]).then_inc(s_mxc, 1)
                emit_stt(12)
            emit_stt(2 * W12)
            phase_end(v, 1)
            phase_wait1(v, 1)
            v.memset(lnsc[:], 1.0 / D)
            v.memset(lneps[:], EPS_LN)
            v.drain()
            v3_st = [0]

            def emit_stt3(budget):
                for _ in range(budget):
                    k3 = v3_st[0]
                    if k3 >= 64:
                        return
                    pwait(v, s_w3ld, k3)
                    if k3 >= 2:
                        pwait(v, s_w3qs, k3 - 2)
                    v.drain()
                    v.tensor_scalar(out=wltB[:], in0=wfB[k3 % 2][:],
                                    scalar1=nthr[:, 2:3], scalar2=None,
                                    op0=ALU.is_lt)
                    v.drain()
                    v.scalar_tensor_tensor(
                        out=wqB[k3 % 2][:], in0=wfB[k3 % 2][:], scalar=thr[:, 2:3],
                        in1=wltB[:], op0=ALU.is_gt, op1=ALU.subtract
                        ).then_inc(s_w3q, 1)
                    v3_st[0] += 1

            for i in range(NT):
                p = i % 2
                pwait(v, s_x, i)
                if i >= 2:
                    v.wait_ge(s_sq, i - 1)
                    pwait(v, s_xns, i - 2)
                v.reduce_sum(out=msum[:, p:p + 1], in_=sx[p][:], axis=AX.X)
                v.drain()
                v.tensor_scalar_mul(out=negm[:, p:p + 1], in0=msum[:, p:p + 1],
                                    scalar1=-1.0 / D)
                v.drain()
                v.tensor_scalar_add(out=sx[p][:], in0=sx[p][:],
                                    scalar1=negm[:, p:p + 1]).then_inc(s_ctf, 1)
                v.wait_ge(s_sq, i + 1)
                v.reciprocal(out=rstd[:, p:p + 1], in_=stdv[:, p:p + 1])
                v.drain()
                v.tensor_scalar_mul(out=xnb[p][:], in0=sx[p][:],
                                    scalar1=rstd[:, p:p + 1]).then_inc(s_xnw, 1)
                emit_stt3(2)
            emit_stt3(64)
            phase_end(v, 2)
            phase_wait1(v, 2)
            for h in range(W12):
                for tsub in range(TS):
                    gi = h * TS + tsub
                    v.wait_ge(s_h2, gi + 1)
                    v.wait_ge(s_sg, gi + 1)
                    if tsub == 0 and h >= 2:
                        pwait(v, s_hst, h - 2)
                    v.tensor_mul(out=hid_sb[h % 2][:, 512 * tsub:512 * (tsub + 1)],
                                 in0=bps[gi % 2][:], in1=sig[gi % 2][:]
                                 ).then_inc(s_hm, 1)
            phase_end(v, 3)
            phase_wait1(v, 3)
            for it in range(8 * TS):
                v.wait_ge(s_omm, it + 1)
                pwait(v, s_xl, it)
                if it >= 2:
                    pwait(v, s_ost, it - 2)
                v.tensor_add(out=osb[it % 2][:, 0:512], in0=ops_[0][:],
                             in1=xs[it % 2][:, 0:512])
                v.tensor_add(out=osb[it % 2][:, 512:1024], in0=ops_[1][:],
                             in1=xs[it % 2][:, 512:1024])
                v.drain()
                v.nop().then_inc(s_oev, 1)

        @blk.tensor
        def _(t_):
            t_.wait_ge(s_ini, 1)
            t_.wait_ge(s_gld, 16)
            t_.matmul(gpsA[:], ones_f[0:8, 0:1], gsb[:], start=True, stop=True
                      ).then_inc(s_g1, 1)
            t_.wait_ge(s_g3, 1)
            t_.matmul(nps[0][:, 0:4], ones_r[:], gtot[:], start=True, stop=True
                      ).then_inc(s_g4, 1)
            t_.wait_ge(s_g6, 1)
            phase_end(t_, 0)
            phase_wait1(t_, 0)
            for tt in range(TS):
                pwait(t_, s_qt, tt)
                if tt >= 2:
                    t_.wait_ge(s_den, tt - 1)
                for c in range(16):
                    mm = t_.matmul(dps[tt % 2][:], lhs_den[:, 64 * c:64 * (c + 1)],
                                   qt[tt % 2][:, c, :],
                                   start=(c == 0), stop=(c == 15))
                mm.then_inc(s_dmm, 1)
                t_.wait_ge(s_den, tt + 1)
                for c in range(16):
                    gi = tt * 16 + c
                    if gi >= 2:
                        t_.wait_ge(s_mxc, gi - 1)
                    t_.matmul(nps[gi % 2][:], lhs_num[:, c, :], qt[tt % 2][:, c, :],
                              start=True, stop=True).then_inc(s_nmm, 1)
                    if gi >= 2:
                        t_.wait_ge(s_bev, gi - 1)
                    t_.matmul(bps[gi % 2][:], ind_sb[:, 128 * c:128 * (c + 1)],
                              den_b[:], start=True, stop=True).then_inc(s_bmm, 1)
            phase_end(t_, 1)
            phase_wait1(t_, 1)
            phase_end(t_, 2)
            phase_wait1(t_, 2)
            t_.wait_ge(s_xnf, 256)
            for h in range(W12):
                pwait(t_, s_l1, h)
                pwait(t_, s_l2, h)
                for tsub in range(TS):
                    gi = h * TS + tsub
                    if gi >= 2:
                        t_.wait_ge(s_sg, gi - 1)
                    for kc in range(16):
                        mm = t_.matmul(nps[gi % 2][:], l1[h % 2][:, kc, :],
                                       xnf[:, T * kc + 512 * tsub:
                                           T * kc + 512 * (tsub + 1)],
                                       start=(kc == 0), stop=(kc == 15))
                    mm.then_inc(s_h1, 1)
                    if gi >= 2:
                        t_.wait_ge(s_hm, gi - 1)
                    for kc in range(16):
                        mm = t_.matmul(bps[gi % 2][:], l2[h % 2][:, kc, :],
                                       xnf[:, T * kc + 512 * tsub:
                                           T * kc + 512 * (tsub + 1)],
                                       start=(kc == 0), stop=(kc == 15))
                    mm.then_inc(s_h2, 1)
            phase_end(t_, 3)
            phase_wait1(t_, 3)
            t_.wait_ge(s_bpr, 16)
            for mh in range(2):
                t_.wait_ge(s_res, 16 * 60 * (mh + 1))
                for tt in range(TS):
                    gmt = mh * TS + tt
                    pwait(t_, s_mxt, gmt)
                    for su in range(4):
                        it = gmt * 4 + su
                        pwait(t_, s_hl, it, k=64)
                        if it >= 1:
                            t_.wait_ge(s_oev, it)
                        for q2 in range(2):
                            for c in range(16):
                                t_.matmul(ops_[q2][:],
                                          mixt[gmt % 2][:, c, 128 * su:128 * (su + 1)],
                                          wpr[:, c, 512 * q2:512 * (q2 + 1)],
                                          start=(c == 0), stop=False)
                            for hc in range(44):
                                t_.matmul(ops_[q2][:],
                                          hidl[it % 2][:, hc, :],
                                          w3r[:, hc, 512 * q2:512 * (q2 + 1)],
                                          start=False, stop=False)
                            mm = t_.matmul(ops_[q2][:], ones_bf[:],
                                           bpr[:, 1024 * mh + 512 * q2:
                                               1024 * mh + 512 * (q2 + 1)],
                                           start=False, stop=True)
                        mm.then_inc(s_omm, 1)
    ctx.close()
    return nc

def _mk_ind():
    ind = np.zeros((64, 2048), np.float32)
    for c in range(16):
        for j in range(4):
            ind[4 * c + j, 128 * c + 32 * j:128 * c + 32 * (j + 1)] = 1.0
    return ind.astype(BF)


def make_A_inmaps(x_flat, wqkv, w1, w2, w3, T, n_cores=8):
    in_maps = []
    for c in range(n_cores):
        gsh = np.stack([
            w1[704 * c:704 * (c + 1), :].reshape(128, GW),
            w2[704 * c:704 * (c + 1), :].reshape(128, GW),
            np.ascontiguousarray(w3[:, 704 * c:704 * (c + 1)]).reshape(128, GW),
        ])
        in_maps.append(dict(x=np.ascontiguousarray(x_flat[T * c:T * (c + 1)]),
                            wqkv=wqkv, gsh=np.ascontiguousarray(gsh)))
    return in_maps


def kernel(L0_attn_x, L0_attn_w_qkv, L0_attn_w_proj, L0_attn_b_proj,
           L0_ffn_w1, L0_ffn_w2, L0_ffn_w3):
    x = np.ascontiguousarray(np.asarray(L0_attn_x, np.float32).reshape(32768, D))
    wqkv = np.ascontiguousarray(np.asarray(L0_attn_w_qkv, np.float32))
    wproj = np.ascontiguousarray(np.asarray(L0_attn_w_proj, np.float32))
    bproj = np.ascontiguousarray(np.asarray(L0_attn_b_proj, np.float32).reshape(1, D))
    w1 = np.ascontiguousarray(np.asarray(L0_ffn_w1, np.float32))
    w2 = np.ascontiguousarray(np.asarray(L0_ffn_w2, np.float32))
    w3 = np.ascontiguousarray(np.asarray(L0_ffn_w3, np.float32))

    T = 4096
    ncA = build_A(T)
    resA = run_bass_kernel_spmd(ncA, make_A_inmaps(x, wqkv, w1, w2, w3, T),
                                core_ids=list(range(8)))
    vkp = np.ascontiguousarray(
        np.stack([np.asarray(resA.results[c]["vkT_out"]) for c in range(8)]))
    gsums = np.zeros((8, 4), np.float32)
    for c in range(8):
        gsums[c, 0:3] = np.asarray(resA.results[c]["gsum_out"])[0]
    ind = _mk_ind()
    ncB = build_B(T)
    in_maps = []
    for c in range(8):
        bselw = np.zeros((32, 8), np.float32)
        bselw[:, 4 * (c // 4):4 * (c // 4) + 4] = 1.0
        in_maps.append(dict(
            x=np.ascontiguousarray(x[T * c:T * (c + 1)]),
            q_in=np.ascontiguousarray(np.asarray(resA.results[c]["q_out"])),
            vkp=vkp, bselw=bselw, gsums=gsums, ind=ind,
            w1=w1, w2=w2, w3=w3, wproj=wproj, bproj=bproj))
    resB = run_bass_kernel_spmd(ncB, in_maps, core_ids=list(range(8)))
    outs = np.concatenate([np.asarray(resB.results[c]["out"]) for c in range(8)], axis=0)
    return outs.reshape(2, 16384, D)
